# version 07
# baseline (speedup 1.0000x reference)
import numpy as np
import ml_dtypes
from contextlib import ExitStack

import concourse.bass as bass
import concourse.mybir as mybir
from concourse.bass_utils import run_bass_kernel_spmd

F32 = mybir.dt.float32
BF16 = mybir.dt.bfloat16
AF = mybir.ActivationFunctionType
ALU = mybir.AluOpType
AX = mybir.AxisListType

T = 4096
D = 2048
DC = D // 128
H = 16
HD = 128
FF = 8192
FC = FF // 128
E = 4
IC = 16
L = 64
NCH = T // L
NTT = T // 128
NTC = T // 512
EPS = 1e-5
RSQRT_HD = float(1.0 / np.sqrt(128.0))


class Sem:
    def __init__(self, st, nc, name):
        self.h = st.enter_context(nc.semaphore(name))
        self.n = 0

    def inc(self, instr, k=1):
        instr.then_inc(self.h, k)
        self.n += k
        return self.n


class Env:
    pass


def _bcast(v):
    v = np.asarray(v, np.float32)
    return np.ascontiguousarray(np.broadcast_to(v[None, :], (128, v.shape[0])))


def _fmcol(v):
    v = np.asarray(v, np.float32)
    return np.ascontiguousarray(v.reshape(-1, 128).T)


def host_constants():
    c = {}
    c["c_ident"] = np.eye(128, dtype=np.float32).astype(ml_dtypes.bfloat16)
    inv = 1.0 / (10000.0 ** (np.arange(0, HD, 2, dtype=np.float64) / HD))
    tpos = np.arange(T, dtype=np.float64)
    ang = tpos[None, :] * inv[:, None]
    cos = np.cos(ang).astype(np.float32)
    sin = np.sin(ang).astype(np.float32)
    c["c_cos"] = np.concatenate([cos, cos], 0).astype(ml_dtypes.bfloat16)
    c["c_sinF"] = np.concatenate([-sin, sin], 0).astype(ml_dtypes.bfloat16)
    masks = np.zeros((4, 128, 512), np.float32)
    kk = np.arange(128)[:, None]
    qq = np.arange(512)[None, :]
    for i, o in enumerate([0, 128, 256, 384]):
        masks[i] = (qq >= o + kk).astype(np.float32)
    c["c_masks"] = masks.astype(ml_dtypes.bfloat16)
    sel = np.zeros((4, 512), np.float32)
    for e in range(4):
        sel[e, e * 128:(e + 1) * 128] = 1.0
    c["c_sel"] = sel.astype(ml_dtypes.bfloat16)
    return c


def param_constants(p):
    c = {}
    c["p_ln1_g"] = _bcast(p["L0_attn_ln1_w"]); c["p_ln1_b"] = _bcast(p["L0_attn_ln1_b"])
    c["p_ln2_g"] = _bcast(p["L0_attn_ln2_w"]); c["p_ln2_b"] = _bcast(p["L0_attn_ln2_b"])
    c["p_fln1_g"] = _bcast(p["L0_ffn_ln1_w"]); c["p_fln1_b"] = _bcast(p["L0_ffn_ln1_b"])
    c["p_fln2_g"] = _bcast(p["L0_ffn_ln2_w"]); c["p_fln2_b"] = _bcast(p["L0_ffn_ln2_b"])
    c["p_in_b"] = _fmcol(p["L0_attn_in_b"])
    c["p_out_b"] = _bcast(p["L0_attn_out_b"])
    c["p_b1"] = _fmcol(p["L0_attn_b1"])
    c["p_b2"] = _bcast(p["L0_attn_b2"])
    c["p_bq"] = _fmcol(p["L0_ffn_bq"]); c["p_bk"] = _fmcol(p["L0_ffn_bk"])
    c["p_bv"] = _fmcol(p["L0_ffn_bv"]); c["p_bo"] = _bcast(p["L0_ffn_bo"])
    c["p_be1"] = np.stack([_fmcol(p["L0_ffn_be1"][e]) for e in range(E)])
    c["p_be2"] = np.ascontiguousarray(np.asarray(p["L0_ffn_be2"], np.float32))
    return c




class Sched:
    def __init__(self, ev, name):
        self.ev = ev
        self.name = name
        self.ops = []
        self.evnum = {}

    def add(self, engine, emit, waits=(), inc=None, sem=None, k=1):
        self.ops.append((engine, emit, tuple(waits), inc, sem, k))

    def dma(self, engine, out, in_, waits=(), inc=None, sem=None):
        assert sem is not None
        if inc is None:
            inc = f"_adma{len(self.ops)}"
        self.add(engine, (lambda eng, o=out, i=in_: eng.dma_start(o, i)),
                 waits, inc, sem, 16)

    def run(self):
        ev = self.ev
        nc = ev.nc
        planned = {}
        for (engine, emit, waits, inc, sem, k) in self.ops:
            if inc is not None:
                base = planned.setdefault(id(sem), sem.n)
                planned[id(sem)] = base + k
                assert inc not in self.evnum, f"dup event {inc}"
                self.evnum[inc] = (sem, planned[id(sem)])
        qs = {}
        for idx, op in enumerate(self.ops):
            qs.setdefault(op[0], []).append(op)
        simval = {}
        for s in (ev.sL, ev.sS, ev.sP, ev.sV, ev.sA, ev.sG, ev.sX):
            simval[id(s)] = s.n
        ptr = {e: 0 for e in qs}
        progress = True
        while progress:
            progress = False
            for e in qs:
                while ptr[e] < len(qs[e]):
                    (engine, emit, waits, inc, sem, k) = qs[e][ptr[e]]
                    ok = True
                    for w in waits:
                        ws, wv = self.evnum[w]
                        if simval[id(ws)] < wv:
                            ok = False
                            break
                    if not ok:
                        break
                    if inc is not None:
                        simval[id(sem)] += k
                    ptr[e] += 1
                    progress = True
        stuck = {e: len(qs[e]) - ptr[e] for e in qs if ptr[e] < len(qs[e])}
        if stuck:
            for e in qs:
                if ptr[e] < len(qs[e]):
                    op = qs[e][ptr[e]]
                    raise RuntimeError(
                        f"Sched {self.name}: deadlock at engine {e}, op#{ptr[e]}, "
                        f"waits={op[2]}, remaining={stuck}")
        with nc.Block() as block:
            def mk(engine_name):
                def emitter(eng):
                    for (engine, emit, waits, inc, sem, k) in self.ops:
                        if engine != engine_name:
                            continue
                        for w in waits:
                            ws, wv = self.evnum[w]
                            eng.wait_ge(ws.h, wv)
                        instr = emit(eng)
                        if inc is not None:
                            if k == 1 and engine in ("vector", "scalar", "gpsimd", "tensor"):
                                eng.maybe_drain_then_inc((sem.h, k), fusable=True)
                                sem.n += k
                            else:
                                sem.inc(instr, k)
                            assert sem.n == self.evnum[inc][1], \
                                f"{self.name}:{inc} planned {self.evnum[inc][1]} got {sem.n}"
                return emitter
            engines_present = {op[0] for op in self.ops}
            if "sync" in engines_present:
                block.sync(mk("sync"))
            if "tensor" in engines_present:
                block.tensor(mk("tensor"))
            if "scalar" in engines_present:
                block.scalar(mk("scalar"))
            if "vector" in engines_present:
                block.vector(mk("vector"))
            if "gpsimd" in engines_present:
                block.gpsimd(mk("gpsimd"))


def phase_prep(ev):
    nc = ev.nc
    with nc.Block() as block:
        @block.sync
        def _(sync):
            ev.sL.inc(sync.dma_start(ev.ident[:], ev.c_ident), 16)
            ev.sL.inc(sync.dma_start(ev.pcol_sb[:, 0:16], ev.p_in_b), 16)
            ev.sL.inc(sync.dma_start(ev.pcol_sb[:, 16:32], ev.p_bq), 16)
            ev.sL.inc(sync.dma_start(ev.pcol_sb[:, 32:48], ev.p_bk), 16)
            ev.sL.inc(sync.dma_start(ev.pcol_sb[:, 48:64], ev.p_bv), 16)
            ev.sL.inc(sync.dma_start(ev.pcol_sb[:, 64:64 + FC], ev.p_b1), 16)
            for e in range(E):
                ev.sL.inc(sync.dma_start(ev.pcol_sb[:, 128 + 16 * e:144 + 16 * e],
                                         ev.p_be1[e]), 16)
            sync.wait_ge(ev.sL.h, ev.sL.n)

        @block.gpsimd
        def _(gpsimd):
            ev.sG.inc(gpsimd.memset(ev.kc_sb[:, 0:1], EPS))


def phase_ln_stats(ev, tag, src):
    nc = ev.nc
    with ExitStack() as ph:
        xin = [ph.enter_context(nc.sbuf_tensor(f"{tag}x{i}", [128, D], F32)) for i in range(3)]
        tmpv = ph.enter_context(nc.sbuf_tensor(f"{tag}v", [128, NTT], F32))
        L0, V0 = ev.sL.n, ev.sV.n
        ld = [L0 + 16 * (tt + 1) for tt in range(NTT)]
        st_done = [V0 + tt + 1 for tt in range(NTT)]
        with nc.Block() as block:
            @block.sync
            def _(sync):
                for tt in range(NTT):
                    if tt >= 3:
                        sync.wait_ge(ev.sV.h, st_done[tt - 3])
                    ev.sL.inc(sync.dma_start(xin[tt % 3][:],
                                             src[tt * 128:(tt + 1) * 128, :]), 16)

            @block.vector
            def _(vector):
                for tt in range(NTT):
                    vector.wait_ge(ev.sL.h, ld[tt])
                    for c in range(4):
                        ins = vector.bn_stats(ev.stat_sb[:, tt * 24 + 6 * c:tt * 24 + 6 * c + 6],
                                              xin[tt % 3][:, 512 * c:512 * (c + 1)])
                        if c == 3:
                            ev.sV.inc(ins)
                vector.wait_ge(ev.sV.h, V0 + NTT)
                for tt in range(NTT):
                    ins = vector.bn_aggr(ev.ag_sb[:, 2 * tt:2 * tt + 2],
                                         ev.stat_sb[:, tt * 24:tt * 24 + 24])
                    if tt == NTT - 1:
                        ev.sV.inc(ins)

            @block.scalar
            def _(scalar):
                scalar.wait_ge(ev.sV.h, V0 + NTT + 1)
                var_ap = bass.AP(ev.ag_sb, 1, [[2 * NTT, 128], [2, NTT]])
                ev.sA.inc(scalar.activation(tmpv[:], var_ap, AF.Ln, bias=ev.kc_sb[:, 0:1]))
                scalar.wait_ge(ev.sA.h, ev.sA.n)
                ev.sA.inc(scalar.activation(ev.istd_sb[:, 0:NTT], tmpv[:], AF.Exp, scale=-0.5))
        ev.stats_marker = ev.sA.n


def phase_ln_apply(ev, tag, src, gains, fm_dst, tm_dst=None, istd2=None):
    nc = ev.nc
    with ExitStack() as ph:
        xin = [ph.enter_context(nc.sbuf_tensor(f"{tag}x{i}", [128, D], F32)) for i in range(2)]
        tn = [ph.enter_context(nc.sbuf_tensor(f"{tag}t{i}", [128, D], F32)) for i in range(2)]
        tg = [ph.enter_context(nc.sbuf_tensor(f"{tag}g{i}", [128, D], F32)) for i in range(1)]
        h1 = [ph.enter_context(nc.sbuf_tensor(f"{tag}h{i}", [128, D], BF16)) for i in range(2)]
        ptr = [ph.enter_context(nc.psum_tensor(f"{tag}p{i}", [128, 1024], BF16)) for i in range(4)]
        ev.gb_sb = ph.enter_context(nc.sbuf_tensor(f"{tag}gb", [128, 2 * D], F32))

        if gains is not None:
            with nc.Block() as block:
                @block.sync
                def _(sync):
                    ev.sL.inc(sync.dma_start(ev.gb_sb[:, 0:D], getattr(ev, gains[0])), 16)
                    ev.sL.inc(sync.dma_start(ev.gb_sb[:, D:2 * D], getattr(ev, gains[1])), 16)
                    sync.wait_ge(ev.sL.h, ev.sL.n)

        L0, V0, G0, P0, S0 = ev.sL.n, ev.sV.n, ev.sG.n, ev.sP.n, ev.sS.n
        ld = [L0 + 16 * (tt + 1) for tt in range(NTT)]
        vevents = {}
        n = V0
        for tt in range(NTT + 2):
            if tt >= 2 and tt - 2 < NTT:
                n += 1; vevents[("cp", tt - 2)] = n
            if tt < NTT:
                n += 1; vevents[("ts", tt)] = n
                if gains is not None:
                    n += 1; vevents[("hb", tt)] = n
        V_end = n
        gevents = {}
        n = G0
        for tt in range(NTT):
            n += 1; gevents[("tg", tt)] = n
        G_end = n
        pevents = {}
        n = P0
        for tt in range(NTT):
            n += 1; pevents[("tr", tt)] = n
        P_end = n

        with nc.Block() as block:
            @block.sync
            def _(sync):
                for tt in range(NTT):
                    if tt >= 2:
                        sync.wait_ge(ev.sV.h, vevents[("ts", tt - 2)])
                    ev.sL.inc(sync.dma_start(xin[tt % 2][:],
                                             src[tt * 128:(tt + 1) * 128, :]), 16)
                assert ev.sL.n == ld[-1]

            @block.vector
            def _(vector):
                for tt in range(NTT + 2):
                    if tt >= 2 and tt - 2 < NTT:
                        t2 = tt - 2
                        vector.wait_ge(ev.sP.h, pevents[("tr", t2)])
                        par = t2 % 2
                        for half in range(2):
                            dst = bass.AP(fm_dst, (8 * half) * T + t2 * 128,
                                          [[DC * T, 128], [T, 8], [1, 128]])
                            ins = vector.tensor_copy(dst, ptr[2 * par + half][:])
                            if half == 1:
                                ev.sV.inc(ins)
                        assert ev.sV.n == vevents[("cp", t2)]
                    if tt < NTT:
                        vector.wait_ge(ev.sL.h, ld[tt])
                        if tt == 0:
                            vector.wait_ge(ev.sA.h, ev.stats_marker)
                        if tt >= 2:
                            vector.wait_ge(ev.sG.h, gevents[("tg", tt - 2)])
                        m_col = ev.ag_sb[:, 2 * tt:2 * tt + 1]
                        i_col = ev.istd_sb[:, tt:tt + 1]
                        if gains is None:
                            ins = vector.tensor_scalar(tn[tt % 2][:], xin[tt % 2][:],
                                                       m_col, i_col,
                                                       op0=ALU.subtract, op1=ALU.mult)
                            ev.sV.inc(ins)
                        else:
                            ins = vector.tensor_scalar(tn[tt % 2][:], xin[tt % 2][:],
                                                       m_col, i_col,
                                                       op0=ALU.subtract, op1=ALU.mult)
                            ev.sV.inc(ins)
                        assert ev.sV.n == vevents[("ts", tt)]
                        if gains is not None:
                            vector.wait_ge(ev.sG.h, gevents[("tg", tt)])
                            if tt >= 2:
                                vector.wait_ge(ev.sP.h, pevents[("tr", tt - 2)])
                            ins = vector.tensor_tensor(h1[tt % 2][:], tg[0][:],
                                                       ev.gb_sb[:, D:2 * D], op=ALU.add)
                            ev.sV.inc(ins)
                            assert ev.sV.n == vevents[("hb", tt)]
                assert ev.sV.n == V_end

            @block.gpsimd
            def _(gpsimd):
                for tt in range(NTT):
                    gpsimd.wait_ge(ev.sV.h, vevents[("ts", tt)])
                    if istd2 is not None:
                        raise NotImplementedError
                    if gains is not None:
                        if tt >= 1:
                            gpsimd.wait_ge(ev.sV.h, vevents[("hb", tt - 1)])
                        ins = gpsimd.tensor_tensor(tg[0][:], tn[tt % 2][:],
                                                   ev.gb_sb[:, 0:D], op=ALU.mult)
                        ev.sG.inc(ins)
                    else:
                        if tt >= 2:
                            gpsimd.wait_ge(ev.sP.h, pevents[("tr", tt - 2)])
                        ins = gpsimd.tensor_copy(h1[tt % 2][:], tn[tt % 2][:])
                        ev.sG.inc(ins)
                    assert ev.sG.n == gevents[("tg", tt)]

            @block.tensor
            def _(tensor):
                for tt in range(NTT):
                    if gains is not None:
                        tensor.wait_ge(ev.sV.h, vevents[("hb", tt)])
                    else:
                        tensor.wait_ge(ev.sG.h, gevents[("tg", tt)])
                    if tt >= 2:
                        tensor.wait_ge(ev.sV.h, vevents[("cp", tt - 2)])
                    par = tt % 2
                    for c in range(DC):
                        ins = tensor.transpose(ptr[2 * par + c // 8][:, 128 * (c % 8):128 * (c % 8 + 1)],
                                               h1[par][:, 128 * c:128 * (c + 1)],
                                               ev.ident[:])
                        if c == DC - 1:
                            ev.sP.inc(ins)
                    assert ev.sP.n == pevents[("tr", tt)]

            if tm_dst is not None:
                @block.scalar
                def _(scalar):
                    for tt in range(NTT):
                        scalar.wait_ge(ev.sV.h, vevents[("ts", tt)])
                        ev.sS.inc(scalar.dma_start(tm_dst[tt * 128:(tt + 1) * 128, :],
                                                   tn[tt % 2][:]), 16)
                    scalar.wait_ge(ev.sS.h, ev.sS.n)


def phase_inproj(ev):
    nc = ev.nc
    with ExitStack() as ph:
        wf = [ph.enter_context(nc.sbuf_tensor(f"ipw{i}", [128, 2048], F32)) for i in range(2)]
        wb = [ph.enter_context(nc.sbuf_tensor(f"ipb{i}", [128, 2048], BF16)) for i in range(2)]
        ub = [ph.enter_context(nc.sbuf_tensor(f"ipu{i}", [128, 512], BF16)) for i in range(4)]
        pp = [ph.enter_context(nc.psum_tensor(f"ipp{i}", [128, 512], F32)) for i in range(8)]

        L0, V0, G0, P0, A0, S0 = ev.sL.n, ev.sV.n, ev.sG.n, ev.sP.n, ev.sA.n, ev.sS.n
        ld = {}
        n = L0
        for hc in range(H):
            n += 16 * DC
            ld[hc] = n
        cast = {hc: G0 + hc + 1 for hc in range(H)}
        mm = {}
        n = P0
        for hc in range(H):
            for half in range(2):
                n += 1
                mm[(hc, half)] = n
        evac = {}
        n = A0
        for hc in range(H):
            for half in range(2):
                for q in range(4):
                    n += 1
                    evac[(hc, half, q)] = n
        stq = {}
        n = S0
        for hc in range(H):
            for half in range(2):
                for q in range(4):
                    n += 16
                    stq[(hc, half, q)] = n

        with nc.Block() as block:
            @block.sync
            def _(sync):
                for hc in range(H):
                    if hc >= 2:
                        sync.wait_ge(ev.sG.h, cast[hc - 2])
                    for dc in range(DC):
                        ev.sL.inc(sync.dma_start(
                            wf[hc % 2][:, 128 * dc:128 * (dc + 1)],
                            ev.in_w[128 * dc:128 * (dc + 1), 128 * hc:128 * (hc + 1)]), 16)
                assert ev.sL.n == ld[H - 1]

            @block.gpsimd
            def _(gpsimd):
                for hc in range(H):
                    gpsimd.wait_ge(ev.sL.h, ld[hc])
                    if hc >= 2:
                        gpsimd.wait_ge(ev.sP.h, mm[(hc - 2, 1)])
                    ev.sG.inc(gpsimd.tensor_copy(wb[hc % 2][:], wf[hc % 2][:]))
                    assert ev.sG.n == cast[hc]

            @block.tensor
            def _(tensor):
                for hc in range(H):
                    tensor.wait_ge(ev.sG.h, cast[hc])
                    for half in range(2):
                        prev = (hc * 2 + half) - 2
                        if prev >= 0:
                            ph_, hh_ = divmod(prev, 2)
                            tensor.wait_ge(ev.sA.h, evac[(ph_, hh_, 3)])
                        for dc in range(DC):
                            lt = wb[hc % 2][:, 128 * dc:128 * (dc + 1)]
                            for q in range(4):
                                tc = half * 4 + q
                                ins = tensor.matmul(
                                    pp[half * 4 + q][:],
                                    lt,
                                    ev.actT[:, dc * T + tc * 512:dc * T + (tc + 1) * 512],
                                    start=(dc == 0), stop=(dc == DC - 1))
                        ev.sP.inc(ins)
                        assert ev.sP.n == mm[(hc, half)]

            @block.scalar
            def _(scalar):
                for hc in range(H):
                    for half in range(2):
                        scalar.wait_ge(ev.sP.h, mm[(hc, half)])
                        for q in range(4):
                            tc = half * 4 + q
                            slot = q
                            prev = (hc, 0, q) if half == 1 else (hc - 1, 1, q)
                            if prev in stq:
                                scalar.wait_ge(ev.sS.h, stq[prev])
                            ins = scalar.activation(ub[slot][:], pp[half * 4 + q][:],
                                                    AF.Identity,
                                                    bias=ev.pcol_sb[:, hc:hc + 1])
                            ev.sA.inc(ins)
                            assert ev.sA.n == evac[(hc, half, q)]
                            scalar.wait_ge(ev.sA.h, ev.sA.n)
                            ev.sS.inc(scalar.dma_start(
                                ev.u_hbm[hc, :, tc * 512:(tc + 1) * 512], ub[slot][:]), 16)
                            assert ev.sS.n == stq[(hc, half, q)]
                scalar.wait_ge(ev.sS.h, ev.sS.n)
    ev.sX.n = ev.sX.n




def phase_ssm(ev):
    nc = ev.nc
    with ExitStack() as ph:
        abcd = ph.enter_context(nc.sbuf_tensor("ssm_abcd", [128, 512], F32))
        bcd_bf = ph.enter_context(nc.sbuf_tensor("ssm_bcd", [128, 384], BF16))
        a_bf = ph.enter_context(nc.sbuf_tensor("ssm_a", [128, 128], BF16))
        BT = ph.enter_context(nc.sbuf_tensor("ssm_BT", [128, 128], BF16))
        CT = ph.enter_context(nc.sbuf_tensor("ssm_CT", [128, 128], BF16))
        DT = ph.enter_context(nc.sbuf_tensor("ssm_DT", [128, 128], BF16))
        ALpl = ph.enter_context(nc.sbuf_tensor("ssm_ALpl", [128, 128], BF16))
        u_sb = ph.enter_context(nc.sbuf_tensor("ssm_u", [128, T], BF16))
        V2 = ph.enter_context(nc.sbuf_tensor("ssm_V2", [128, T], BF16))
        hT = ph.enter_context(nc.sbuf_tensor("ssm_hT", [128, T], BF16))
        ApT = ph.enter_context(nc.sbuf_tensor("ssm_ApT", [128, 64 * 128], BF16))
        AlT = ph.enter_context(nc.sbuf_tensor("ssm_AlT", [128, 63 * 128], BF16))
        Sloc = ph.enter_context(nc.sbuf_tensor("ssm_Sloc", [128, 64], BF16))
        Hsb = ph.enter_context(nc.sbuf_tensor("ssm_Hsb", [128, 64], BF16))
        yb = [ph.enter_context(nc.sbuf_tensor(f"ssm_y{i}", [128, 512], BF16)) for i in range(2)]
        pc = [ph.enter_context(nc.psum_tensor(f"ssm_p{i}", [128, 512], F32)) for i in range(8)]

        S = Sched(ev, "ssm")
        SWAP_OUT = lambda dst, off: bass.AP(dst, off, [[T, 128], [1, 8], [64, 64]])
        PS_3D = lambda p: bass.AP(p, 0, [[512, 128], [64, 8], [1, 64]])

        def ap_slice(t, c0, c1):
            return t[:, c0:c1]

        for h in range(H):
            hp = f"h{h}"
            pv = f"h{h - 1}"

            w = []
            if h >= 1:
                w = [pv + "ymm7"]
            for j, src in enumerate([ev.A, ev.Bm, ev.Cm, ev.Dm]):
                S.dma("sync", abcd[:, 128 * j:128 * (j + 1)], src[h],
                      waits=(w if j == 0 else ()),
                      inc=(hp + "ld_abcd" if j == 3 else None), sem=ev.sL)
            S.dma("sync", u_sb[:], ev.u_hbm[h], waits=w,
                  inc=hp + "ld_u", sem=ev.sL)

            w = [hp + "ld_abcd"]
            if h >= 1:
                w += [pv + "chmm64", pv + "trE"]
            S.add("gpsimd", lambda eng, s=abcd, d=a_bf: eng.tensor_copy(d[:], s[:, 0:128]),
                  waits=w, inc=None)
            S.add("gpsimd", lambda eng, s=abcd, d=bcd_bf: eng.tensor_copy(d[:], s[:, 128:512]),
                  waits=(), inc=hp + "cast", sem=ev.sG)

            w3 = [hp + "cast"]
            if h >= 1:
                w3 += [pv + "evA_done"]
            for j, evn in enumerate(["trB", "trC", "trD"]):
                S.add("tensor", lambda eng, j=j: eng.matmul(
                    pc[4 + j][:, 0:128], bcd_bf[:, 128 * j:128 * (j + 1)], ev.ident[:],
                    start=True, stop=True),
                    waits=(w3 if j == 0 else ()), inc=hp + evn, sem=ev.sP)

            w4 = [hp + "trD"]
            if h >= 1:
                w4 += [pv + "ymm7"]
            for j, (dst, evn) in enumerate([(BT, "trEb"), (CT, "trEc"), (DT, "trE")]):
                S.add("vector", lambda eng, j=j, dst=dst: eng.tensor_copy(
                    dst[:], pc[4 + j][:, 0:128]),
                    waits=(w4 if j == 0 else ()),
                    inc=(hp + "trE" if j == 2 else None), sem=ev.sV)

            for tc in range(NTC):
                w5 = [hp + "trE", hp + "ld_u"]
                if tc >= 2:
                    w5 = [hp + f"vev{tc - 2}"]
                elif h >= 1:
                    w5 += [pv + "evB_done"]
                S.add("tensor", lambda eng, tc=tc: eng.matmul(
                    pc[2 + tc % 2][:], BT[:], u_sb[:, 512 * tc:512 * (tc + 1)],
                    start=True, stop=True),
                    waits=w5, inc=hp + f"vmm{tc}", sem=ev.sP)
                w6 = [hp + f"vmm{tc}"]
                if tc == 0 and h >= 1:
                    w6 += [pv + "convB_done"]
                S.add("vector", lambda eng, tc=tc: eng.tensor_copy(
                    SWAP_OUT(V2, tc * 8), PS_3D(pc[2 + tc % 2])),
                    waits=w6, inc=hp + f"vev{tc}", sem=ev.sV)

            for d in range(1, 65):
                rhs = ev.ident[:] if d == 1 else ApT[:, 128 * (d - 2):128 * (d - 1)]
                w7 = []
                if d == 1:
                    w7 = [hp + "cast"]
                    if h >= 1:
                        w7 += [pv + "evB_done"]
                else:
                    w7 = [hp + f"chev{d - 1}"]
                if d >= 3:
                    w7 += [hp + f"chev{d - 2}"]
                S.add("tensor", lambda eng, d=d, rhs=rhs: eng.matmul(
                    pc[d % 2][:, 0:128], a_bf[:], rhs, start=True, stop=True),
                    waits=w7, inc=hp + f"chmm{d}", sem=ev.sP)
                w7e = [hp + f"chmm{d}"]
                if h >= 1 and d <= 2:
                    w7e += [pv + "convB_done"]
                S.add("vector", lambda eng, d=d: eng.tensor_copy(
                    ApT[:, 128 * (d - 1):128 * d], pc[d % 2][:, 0:128]),
                    waits=w7e, inc=hp + f"chev{d}", sem=ev.sV)
            S.add("tensor", lambda eng: eng.matmul(
                pc[1][:, 0:128], ApT[:, 63 * 128:64 * 128], ev.ident[:],
                start=True, stop=True),
                waits=[hp + "chev64"], inc=hp + "almm0", sem=ev.sP)
            S.add("vector", lambda eng: eng.tensor_copy(ALpl[:], pc[1][:, 0:128]),
                  waits=[hp + "almm0"], inc=hp + "alev0", sem=ev.sV)
            for dd in range(1, 64):
                rhs = ev.ident[:] if dd == 1 else AlT[:, 128 * (dd - 2):128 * (dd - 1)]
                w7 = [hp + ("alev0" if dd == 1 else f"alev{dd - 1}")]
                if dd >= 2:
                    w7 += [hp + f"alev{dd - 2}" if dd >= 3 else hp + "chev64"]
                S.add("tensor", lambda eng, dd=dd, rhs=rhs: eng.matmul(
                    pc[dd % 2][:, 0:128], ALpl[:], rhs, start=True, stop=True),
                    waits=w7, inc=hp + f"almm{dd}", sem=ev.sP)
                S.add("vector", lambda eng, dd=dd: eng.tensor_copy(
                    AlT[:, 128 * (dd - 1):128 * dd], pc[dd % 2][:, 0:128]),
                    waits=[hp + f"almm{dd}"], inc=hp + f"alev{dd}", sem=ev.sV)

            first = True
            for d in range(64):
                lhsT = ev.ident[:] if d == 0 else ApT[:, 128 * (d - 1):128 * d]
                s_out = max(d, 32) * 64
                for b in range(s_out // 512, 8):
                    o0 = max(s_out, 512 * b); o1 = 512 * (b + 1)
                    w8 = []
                    if first:
                        w8 = [hp + "vev7", hp + "alev63"]
                        if h >= 1:
                            w8 += [pv + "evA_done"]
                        first = False
                    S.add("tensor", lambda eng, d=d, b=b, o0=o0, o1=o1, lhsT=lhsT: eng.matmul(
                        pc[b][:, o0 - 512 * b:o1 - 512 * b], lhsT,
                        V2[:, o0 - d * 64:o1 - d * 64],
                        start=(d == 0), stop=False),
                        waits=w8,
                        inc=(hp + "convA_done" if (d == 63 and b == 7) else None),
                        sem=ev.sP)
            w9 = [hp + "convA_done"]
            if h >= 1:
                w9 += [pv + "toep_done"]
            S.add("vector", lambda eng: eng.tensor_copy(Sloc[:], pc[7][:, 448:512]),
                  waits=w9, inc=hp + "slocev", sem=ev.sV)
            for dd in range(63):
                lhsT = ev.ident[:] if dd == 0 else AlT[:, 128 * (dd - 1):128 * dd]
                S.add("tensor", lambda eng, dd=dd, lhsT=lhsT: eng.matmul(
                    pc[0][:, dd + 1:64], lhsT, Sloc[:, 0:63 - dd],
                    start=(dd == 0), stop=(dd == 62)),
                    waits=([hp + "slocev"] if dd == 0 else ()),
                    inc=(hp + "toep_done" if dd == 62 else None), sem=ev.sP)
            wz = []
            if h >= 1:
                wz = [pv + "convB_done"]
            S.add("gpsimd", lambda eng: eng.memset(Hsb[:, 0:1], 0.0),
                  waits=wz, inc=hp + "hz", sem=ev.sG)
            S.add("vector", lambda eng: eng.tensor_copy(Hsb[:, 1:64], pc[0][:, 1:64]),
                  waits=[hp + "toep_done", hp + "hz"], inc=hp + "hev", sem=ev.sV)
            for p in range(32, 64):
                S.add("tensor", lambda eng, p=p: eng.matmul(
                    pc[p // 8][:, (p * 64) % 512:(p * 64) % 512 + 64],
                    ApT[:, 128 * p:128 * (p + 1)], Hsb[:, 0:64],
                    start=False, stop=(p % 8 == 7)),
                    waits=([hp + "hev"] if p == 32 else ()),
                    inc=(hp + "corrA_done" if p == 63 else None), sem=ev.sP)
            for b in range(4, 8):
                w14 = [hp + "corrA_done"] if b == 4 else ()
                S.add("vector", lambda eng, b=b: eng.tensor_copy(
                    bass.AP(hT, 8 * b, [[T, 128], [1, 8], [64, 64]]), PS_3D(pc[b])),
                    waits=w14, inc=(hp + "evA_done" if b == 7 else None), sem=ev.sV)
            for dd in range(33):
                lhsT = ev.ident[:] if dd == 0 else ApT[:, 128 * (dd - 1):128 * dd]
                if dd <= 31:
                    s_out = dd * 64
                    for b in range(s_out // 512, 4):
                        o0 = max(s_out, 512 * b); o1 = 512 * (b + 1)
                        S.add("tensor", lambda eng, dd=dd, b=b, o0=o0, o1=o1, lhsT=lhsT:
                              eng.matmul(pc[b][:, o0 - 512 * b:o1 - 512 * b], lhsT,
                                         V2[:, o0 - dd * 64:o1 - dd * 64],
                                         start=(dd == 0), stop=False),
                              waits=([hp + "hev"] if dd == 0 and b == 0 else ()),
                              inc=None, sem=ev.sP)
                if dd >= 1:
                    p = dd - 1
                    S.add("tensor", lambda eng, p=p, lhsT=lhsT: eng.matmul(
                        pc[p // 8][:, (p * 64) % 512:(p * 64) % 512 + 64],
                        lhsT, Hsb[:, 0:64], start=False, stop=(p % 8 == 7)),
                        waits=(), inc=(hp + "convB_done" if p == 31 else None),
                        sem=ev.sP)
            for b in range(4):
                w16 = [hp + "convB_done"] if b == 0 else ()
                S.add("vector", lambda eng, b=b: eng.tensor_copy(
                    bass.AP(hT, 8 * b, [[T, 128], [1, 8], [64, 64]]), PS_3D(pc[b])),
                    waits=w16, inc=(hp + "evB_done" if b == 3 else None), sem=ev.sV)
            for tc in range(NTC):
                w17 = [hp + "evB_done"] if tc == 0 else []
                if tc >= 2:
                    w17 += [hp + f"yev{tc - 2}"]
                S.add("tensor", lambda eng, tc=tc: eng.matmul(
                    pc[4 + tc % 2][:], CT[:], hT[:, 512 * tc:512 * (tc + 1)],
                    start=True, stop=False), waits=w17, inc=None, sem=ev.sP)
                S.add("tensor", lambda eng, tc=tc: eng.matmul(
                    pc[4 + tc % 2][:], DT[:], u_sb[:, 512 * tc:512 * (tc + 1)],
                    start=False, stop=True), waits=(), inc=hp + f"ymm{tc}", sem=ev.sP)
                wev = [hp + f"ymm{tc}"]
                if tc >= 2:
                    wev += [hp + f"yst{tc - 2}"]
                S.add("scalar", lambda eng, tc=tc: eng.activation(
                    yb[tc % 2][:], pc[4 + tc % 2][:], AF.Identity),
                    waits=wev, inc=hp + f"yev{tc}", sem=ev.sA)
                for j in range(4):
                    tt = tc * 4 + j
                    S.dma("scalar", ev.y_hbm[tt, :, h * 128:(h + 1) * 128],
                          yb[tc % 2][:, 128 * j:128 * (j + 1)],
                          waits=([hp + f"yev{tc}"] if j == 0 else ()),
                          inc=(hp + f"yst{tc}" if j == 3 else None), sem=ev.sS)
        S.add("scalar", lambda eng: eng.nop(), waits=[f"h{H - 1}yst7"], inc=None)
        S.run()




def phase_proj_tm(ev, tag, stat_src, wmov, dout, res, dst, bias_bc, do_stats=True):
    nc = ev.nc
    KC = DC
    OG = dout // 512
    with ExitStack() as ph:
        wof = [ph.enter_context(nc.sbuf_tensor(f"{tag}wf{i}", [128, dout], F32)) for i in range(2)]
        wob = ph.enter_context(nc.sbuf_tensor(f"{tag}wb", [128, KC * dout], BF16))
        obc = ph.enter_context(nc.sbuf_tensor(f"{tag}ob", [128, D], F32))
        ysb = [ph.enter_context(nc.sbuf_tensor(f"{tag}y{i}", [128, 2048], BF16)) for i in range(2)]
        r1 = [ph.enter_context(nc.sbuf_tensor(f"{tag}r1_{i}", [128, D], F32)) for i in range(2)]
        r2 = [ph.enter_context(nc.sbuf_tensor(f"{tag}r2_{i}", [128, D], F32)) for i in range(2)]
        tband = [ph.enter_context(nc.sbuf_tensor(f"{tag}tb{i}", [128, 512], F32)) for i in range(4)]
        oband = [ph.enter_context(nc.sbuf_tensor(f"{tag}obd{i}", [128, 512], F32)) for i in range(4)]
        pp = [ph.enter_context(nc.psum_tensor(f"{tag}pp{i}", [128, 512], F32)) for i in range(8)]
        tmpv = ph.enter_context(nc.sbuf_tensor(f"{tag}tv", [128, NTT], F32))

        S = Sched(ev, tag)
        for kc in range(KC):
            w = []
            if kc >= 2:
                w = [f"{tag}wc{kc - 2}"]
            S.dma("sync", wof[kc % 2][:], wmov[128 * kc:128 * (kc + 1), 0:dout],
                  waits=w, inc=f"{tag}wl{kc}", sem=ev.sL)
            S.add("gpsimd", lambda eng, kc=kc: eng.tensor_copy(
                wob[:, kc * dout:(kc + 1) * dout], wof[kc % 2][:]),
                waits=[f"{tag}wl{kc}"], inc=f"{tag}wc{kc}", sem=ev.sG)
        S.dma("sync", obc[:], bias_bc, inc=f"{tag}obl", sem=ev.sL)

        src_kind, src_t = stat_src[0], stat_src[1]
        for tt in range(NTT):
            if src_kind == "hbm_tiles":
                w = []
                if tt >= 2:
                    w = [f"{tag}mm{tt - 2}_{OG - 1}"]
                S.dma("sync", ysb[tt % 2][:], src_t[tt], waits=w,
                      inc=f"{tag}yl{tt}", sem=ev.sL)
            for j, (rt, rbuf) in enumerate(zip(res, (r1, r2))):
                w = []
                if tt >= 2:
                    w = [f"{tag}pl{tt - 2}_{OG - 1}"] if j == 0 else [f"{tag}f2{tt - 2}_{OG - 1}"]
                S.dma("sync", rbuf[tt % 2][:], rt[tt * 128:(tt + 1) * 128, :],
                      waits=w, inc=f"{tag}rl{j}_{tt}", sem=ev.sL)
            for og in range(OG):
                w = []
                if tt == 0 and og == 0:
                    w = [f"{tag}wc{KC - 1}"]
                    if src_kind == "hbm_tiles":
                        w += [f"{tag}yl0"]
                elif src_kind == "hbm_tiles" and og == 0:
                    w = [f"{tag}yl{tt}"]
                if tt >= 2 and og == 0:
                    w += [f"{tag}ev{tt - 2}_{OG - 1}"]
                for hc in range(KC):
                    if src_kind == "hbm_tiles":
                        lhsT = ysb[tt % 2][:, 128 * hc:128 * (hc + 1)]
                    else:
                        lhsT = src_t[:, hc * stat_src[2] + tt * 128:hc * stat_src[2] + tt * 128 + 128]
                    S.add("tensor", lambda eng, lhsT=lhsT, og=og, hc=hc, tt=tt: eng.matmul(
                        pp[(tt % 2) * 4 + og][:], lhsT,
                        wob[:, hc * dout + og * 512:hc * dout + og * 512 + 512],
                        start=(hc == 0), stop=(hc == KC - 1)),
                        waits=(w if hc == 0 else ()),
                        inc=(f"{tag}mm{tt}_{og}" if hc == KC - 1 else None), sem=ev.sP)
                w = [f"{tag}mm{tt}_{og}"]
                if tt == 0 and og == 0:
                    w += [f"{tag}obl"]
                if tt >= 1:
                    if len(res) >= 1:
                        w += [f"{tag}pl{tt - 1}_{og}"]
                    else:
                        w += [f"{tag}sd{tt - 1}_{og}"]
                S.add("vector", lambda eng, tt=tt, og=og: eng.scalar_tensor_tensor(
                    tband[og][:], pp[(tt % 2) * 4 + og][:], 1.0,
                    obc[:, og * 512:(og + 1) * 512], op0=ALU.mult, op1=ALU.add),
                    waits=w, inc=f"{tag}ev{tt}_{og}", sem=ev.sV)
                if len(res) >= 1:
                    w = [f"{tag}ev{tt}_{og}", f"{tag}rl0_{tt}"]
                    if tt >= 1:
                        w += [f"{tag}sd{tt - 1}_{og}"]
                    S.add("gpsimd", lambda eng, tt=tt, og=og: eng.tensor_tensor(
                        oband[og][:], tband[og][:],
                        r1[tt % 2][:, og * 512:(og + 1) * 512], op=ALU.add),
                        waits=w, inc=f"{tag}pl{tt}_{og}", sem=ev.sG)
                    fin_buf, fin_ev = oband, f"{tag}pl{tt}_{og}"
                    fin_eng = "gpsimd"
                else:
                    fin_buf, fin_ev = tband, f"{tag}ev{tt}_{og}"
                    fin_eng = "vector"
                if len(res) >= 2:
                    w = [f"{tag}pl{tt}_{og}", f"{tag}rl1_{tt}"]
                    if tt >= 1:
                        w += [f"{tag}sd{tt - 1}_{og}"]
                    S.add("vector", lambda eng, tt=tt, og=og: eng.tensor_tensor(
                        tband[og][:], oband[og][:],
                        r2[tt % 2][:, og * 512:(og + 1) * 512], op=ALU.add),
                        waits=w, inc=f"{tag}f2{tt}_{og}", sem=ev.sV)
                    fin_buf, fin_ev = tband, f"{tag}f2{tt}_{og}"
                    fin_eng = "vector"
                if do_stats:
                    w = [fin_ev]
                    S.add("vector", lambda eng, tt=tt, og=og, fb=fin_buf: eng.bn_stats(
                        ev.stat_sb[:, tt * 24 + 6 * og:tt * 24 + 6 * og + 6], fb[og][:]),
                        waits=w, inc=f"{tag}bs{tt}_{og}", sem=ev.sV)
                S.dma("scalar", dst[tt * 128:(tt + 1) * 128, og * 512:(og + 1) * 512],
                      fin_buf[og][:], waits=[fin_ev], inc=f"{tag}sd{tt}_{og}", sem=ev.sS)
        if do_stats:
            S.add("vector", lambda eng: eng.nop(), waits=[f"{tag}bs{NTT - 1}_{OG - 1}"],
                  inc=f"{tag}bsall", sem=ev.sV)
            for tt in range(NTT):
                S.add("vector", lambda eng, tt=tt: eng.bn_aggr(
                    ev.ag_sb[:, 2 * tt:2 * tt + 2], ev.stat_sb[:, tt * 24:tt * 24 + 24]),
                    waits=([f"{tag}bsall"] if tt == 0 else ()),
                    inc=(f"{tag}agg" if tt == NTT - 1 else None), sem=ev.sV)
            S.add("scalar", lambda eng: eng.activation(
                tmpv[:], bass.AP(ev.ag_sb, 1, [[2 * NTT, 128], [2, NTT]]), AF.Ln,
                bias=ev.kc_sb[:, 0:1]),
                waits=[f"{tag}agg"], inc=f"{tag}ln", sem=ev.sA)
            S.add("scalar", lambda eng: eng.activation(
                ev.istd_sb[:, 0:NTT], tmpv[:], AF.Exp, scale=-0.5),
                waits=[f"{tag}ln"], inc=f"{tag}istd", sem=ev.sA)
        S.add("scalar", lambda eng: eng.nop(), waits=[f"{tag}sd{NTT - 1}_{OG - 1}"], inc=None)
        S.run()
        if do_stats:
            ev.stats_marker = ev.sA.n




def phase_mlp_up(ev):
    nc = ev.nc
    with ExitStack() as ph:
        wf = [ph.enter_context(nc.sbuf_tensor(f"muw{i}", [128, 2048], F32)) for i in range(2)]
        wb = [ph.enter_context(nc.sbuf_tensor(f"mub{i}", [128, 2048], BF16)) for i in range(2)]
        mb = [ph.enter_context(nc.sbuf_tensor(f"mum{i}", [128, 512], BF16)) for i in range(4)]
        c2f = [ph.enter_context(nc.sbuf_tensor(f"muc{i}", [128, 2048], F32)) for i in range(2)]
        c2b = [ph.enter_context(nc.sbuf_tensor(f"mud{i}", [128, 2048], BF16)) for i in range(2)]
        pp = [ph.enter_context(nc.psum_tensor(f"mup{i}", [128, 512], F32)) for i in range(8)]
        S = Sched(ev, "mup")
        for fc in range(FC):
            w = []
            if fc >= 2:
                w = [f"ct{fc - 2}"]
            for dc in range(DC):
                S.dma("sync", wf[fc % 2][:, 128 * dc:128 * (dc + 1)],
                      ev.w1[128 * dc:128 * (dc + 1), 128 * fc:128 * (fc + 1)],
                      waits=(w if dc == 0 else ()),
                      inc=(f"wl{fc}" if dc == DC - 1 else None), sem=ev.sL)
            w = [f"wl{fc}"]
            if fc >= 2:
                w += [f"mm{fc - 2}_1"]
            S.add("gpsimd", lambda eng, fc=fc: eng.tensor_copy(wb[fc % 2][:], wf[fc % 2][:]),
                  waits=w, inc=f"ct{fc}", sem=ev.sG)
            for half in range(2):
                for dc in range(DC):
                    w = []
                    if dc == 0:
                        w = [f"ct{fc}"]
                        if fc >= 1:
                            w += [f"ge{fc - 1}_{half}_3"]
                    for q in range(4):
                        tc = half * 4 + q
                        S.add("tensor", lambda eng, fc=fc, dc=dc, q=q, tc=tc, half=half:
                              eng.matmul(pp[half * 4 + q][:],
                                         wb[fc % 2][:, 128 * dc:128 * (dc + 1)],
                                         ev.actT[:, dc * T + tc * 512:dc * T + (tc + 1) * 512],
                                         start=(dc == 0), stop=(dc == DC - 1)),
                              waits=(w if q == 0 else ()),
                              inc=(f"mm{fc}_{half}" if dc == DC - 1 and q == 3 else None),
                              sem=ev.sP)
                for q in range(4):
                    tc = half * 4 + q
                    w = [f"mm{fc}_{half}"] if q == 0 else []
                    if fc >= 1:
                        w += [f"ms{fc - 1}_{half}_{q}"]
                    S.add("scalar", lambda eng, fc=fc, q=q, half=half: eng.activation(
                        mb[q][:], pp[half * 4 + q][:], AF.Gelu,
                        bias=ev.pcol_sb[:, 64 + fc:65 + fc]),
                        waits=w, inc=f"ge{fc}_{half}_{q}", sem=ev.sA)
                    S.dma("scalar", ev.mid_hbm[fc, :, tc * 512:(tc + 1) * 512], mb[q][:],
                          waits=[f"ge{fc}_{half}_{q}"], inc=f"ms{fc}_{half}_{q}", sem=ev.sS)
        for fc in range(FC):
            w = []
            if fc >= 2:
                w = [f"w2c{fc - 2}"]
            S.dma("sync", c2f[fc % 2][:], ev.w2[128 * fc:128 * (fc + 1), :],
                  waits=w, inc=f"w2l{fc}", sem=ev.sL)
            w = [f"w2l{fc}"]
            if fc >= 2:
                w += [f"w2s{fc - 2}"]
            S.add("gpsimd", lambda eng, fc=fc: eng.tensor_copy(c2b[fc % 2][:], c2f[fc % 2][:]),
                  waits=w, inc=f"w2c{fc}", sem=ev.sG)
            S.dma("scalar", ev.w2bf_hbm[fc], c2b[fc % 2][:],
                  waits=[f"w2c{fc}"], inc=f"w2s{fc}", sem=ev.sS)
        S.add("scalar", lambda eng: eng.nop(),
              waits=[f"ms{FC - 1}_1_3", f"w2s{FC - 1}"], inc=None)
        S.run()


def phase_mlp_down(ev):
    nc = ev.nc
    NTC_ = 8
    with ExitStack() as ph:
        mc = ph.enter_context(nc.sbuf_tensor("mdm", [128, FC * 512], BF16))
        w2t = [ph.enter_context(nc.sbuf_tensor(f"mdw{i}", [128, 512], BF16)) for i in range(8)]
        obc = ph.enter_context(nc.sbuf_tensor("mdob", [128, D], F32))
        r1 = [ph.enter_context(nc.sbuf_tensor(f"mdr1_{i}", [128, 512], F32)) for i in range(3)]
        r2 = [ph.enter_context(nc.sbuf_tensor(f"mdr2_{i}", [128, 512], F32)) for i in range(3)]
        tband = [ph.enter_context(nc.sbuf_tensor(f"mdt{i}", [128, 512], F32)) for i in range(3)]
        oband = [ph.enter_context(nc.sbuf_tensor(f"mdo{i}", [128, 512], F32)) for i in range(3)]
        sband = [ph.enter_context(nc.sbuf_tensor(f"mds{i}", [128, 512], F32)) for i in range(3)]
        pp = [ph.enter_context(nc.psum_tensor(f"mdp{i}", [128, 512], F32)) for i in range(8)]
        tmpv = ph.enter_context(nc.sbuf_tensor("mdtv", [128, NTT], F32))
        S = Sched(ev, "mdn")
        S.dma("sync", obc[:], ev.p_b2, inc="obl", sem=ev.sL)
        slot = 0
        for tc in range(NTC_):
            for fc in range(FC):
                w = []
                if fc == 0 and tc >= 1:
                    gl = ((tc - 1) * 4 + 3) * FC + FC - 1
                    w = [f"mm_w{gl}"]
                S.dma("sync", mc[:, fc * 512:(fc + 1) * 512],
                      ev.mid_hbm[fc, :, tc * 512:(tc + 1) * 512],
                      waits=w, inc=(f"ml{tc}" if fc == FC - 1 else None), sem=ev.sL)
            for og in range(4):
                for fc in range(FC):
                    w = []
                    gi = (tc * 4 + og) * FC + fc
                    if gi >= 8:
                        w = [f"mm_w{gi - 7}"]
                    S.dma("sync", w2t[gi % 8][:],
                          ev.w2bf_hbm[fc, :, og * 512:(og + 1) * 512],
                          waits=w, inc=f"wt{gi}", sem=ev.sL)
                    gog = tc * 4 + og
                    for tt in range(4):
                        w2 = []
                        if fc == 0 and tt == 0:
                            w2 = [f"ml{tc}", f"wt{gi}"]
                            if gog >= 2:
                                w2 += [f"ev_{(gog - 1) * 4 + 0}"]
                        elif tt == 0:
                            w2 = [f"wt{gi}"]
                        S.add("tensor", lambda eng, gog=gog, fc=fc, tt=tt, gi=gi:
                              eng.matmul(pp[(gog % 2) * 4 + tt][:],
                                         mc[:, fc * 512 + tt * 128:fc * 512 + tt * 128 + 128],
                                         w2t[gi % 8][:],
                                         start=(fc == 0), stop=(fc == FC - 1)),
                              waits=w2,
                              inc=(f"mm_w{gi}" if tt == 3 else None),
                              sem=ev.sP)
                for tt in range(4):
                    gtt = tc * 4 + tt
                    w = [f"mm_w{(tc * 4 + og) * FC + FC - 1}"]
                    if gtt == 0 and og == 0:
                        w += ["obl"]
                    sl = slot % 3
                    S.dma("sync", r1[sl][:],
                          ev.x1_hbm[gtt * 128:(gtt + 1) * 128, og * 512:(og + 1) * 512],
                          waits=([f"pl_{slot - 3}"] if slot >= 3 else ()),
                          inc=f"rl1_{gtt}_{og}", sem=ev.sL)
                    S.dma("sync", r2[sl][:],
                          ev.x[gtt * 128:(gtt + 1) * 128, og * 512:(og + 1) * 512],
                          waits=([f"f2_{slot - 3}"] if slot >= 3 else ()),
                          inc=f"rl2_{gtt}_{og}", sem=ev.sL)
                    if slot >= 3:
                        w += [f"pl_{slot - 3}"]
                    S.add("vector", lambda eng, tc=tc, tt=tt, og=og, sl=sl: eng.scalar_tensor_tensor(
                        tband[sl][:], pp[((tc * 4 + og) % 2) * 4 + tt][:], 1.0,
                        obc[:, og * 512:(og + 1) * 512], op0=ALU.mult, op1=ALU.add),
                        waits=w, inc=f"ev_{slot}", sem=ev.sV)
                    w = [f"ev_{slot}", f"rl1_{gtt}_{og}"]
                    if slot >= 3:
                        w += [f"f2_{slot - 3}"]
                    S.add("gpsimd", lambda eng, sl=sl: eng.tensor_tensor(
                        oband[sl][:], tband[sl][:], r1[sl][:], op=ALU.add),
                        waits=w, inc=f"pl_{slot}", sem=ev.sG)
                    w = [f"pl_{slot}", f"rl2_{gtt}_{og}"]
                    if slot >= 3:
                        w += [f"sd_{slot - 3}", f"bs_{slot - 3}"]
                    S.add("vector", lambda eng, sl=sl: eng.tensor_tensor(
                        sband[sl][:], oband[sl][:], r2[sl][:], op=ALU.add),
                        waits=w, inc=f"f2_{slot}", sem=ev.sV)
                    S.add("vector", lambda eng, gtt=gtt, og=og, sl=sl: eng.bn_stats(
                        ev.stat_sb[:, gtt * 24 + 6 * og:gtt * 24 + 6 * og + 6], sband[sl][:]),
                        waits=[f"f2_{slot}"], inc=f"bs_{slot}", sem=ev.sV)
                    S.dma("scalar", ev.s1_hbm[gtt * 128:(gtt + 1) * 128,
                                              og * 512:(og + 1) * 512],
                          sband[sl][:], waits=[f"f2_{slot}"], inc=f"sd_{slot}", sem=ev.sS)
                    slot += 1
        S.add("vector", lambda eng: eng.nop(), waits=[f"bs_{slot - 1}"],
              inc="bsall", sem=ev.sV)
        for tt in range(NTT):
            S.add("vector", lambda eng, tt=tt: eng.bn_aggr(
                ev.ag_sb[:, 2 * tt:2 * tt + 2], ev.stat_sb[:, tt * 24:tt * 24 + 24]),
                waits=(["bsall"] if tt == 0 else ()),
                inc=("agg" if tt == NTT - 1 else None), sem=ev.sV)
        S.add("scalar", lambda eng: eng.activation(
            tmpv[:], bass.AP(ev.ag_sb, 1, [[2 * NTT, 128], [2, NTT]]), AF.Ln,
            bias=ev.kc_sb[:, 0:1]), waits=["agg"], inc="lnt", sem=ev.sA)
        S.add("scalar", lambda eng: eng.activation(
            ev.istd_sb[:, 0:NTT], tmpv[:], AF.Exp, scale=-0.5),
            waits=["lnt"], inc="istd", sem=ev.sA)
        S.add("scalar", lambda eng: eng.nop(), waits=[f"sd_{slot - 1}"], inc=None)
        S.run()
        ev.stats_marker = ev.sA.n




def phase_ln_merged(ev, tag, src, gains, fm_dst, tm_dst):
    nc = ev.nc
    with ExitStack() as ph:
        xin = [ph.enter_context(nc.sbuf_tensor(f"{tag}x{i}", [128, D], F32)) for i in range(2)]
        tn = [ph.enter_context(nc.sbuf_tensor(f"{tag}t{i}", [128, D], F32)) for i in range(2)]
        t2b = [ph.enter_context(nc.sbuf_tensor(f"{tag}u{i}", [128, D], F32)) for i in range(2)]
        h1 = [ph.enter_context(nc.sbuf_tensor(f"{tag}h{i}", [128, D], BF16)) for i in range(2)]
        ptr = [ph.enter_context(nc.psum_tensor(f"{tag}p{i}", [128, 1024], BF16)) for i in range(4)]
        vv = ph.enter_context(nc.sbuf_tensor(f"{tag}vv", [128, 4 * NTT], F32))
        ev.gb_sb = ph.enter_context(nc.sbuf_tensor(f"{tag}gb", [128, 2 * D], F32))

        S = Sched(ev, tag)
        S.dma("sync", ev.gb_sb[:, 0:D], getattr(ev, gains[0]), inc="gl", sem=ev.sL)
        S.dma("sync", ev.gb_sb[:, D:2 * D], getattr(ev, gains[1]), inc="bl", sem=ev.sL)
        vview = bass.AP(ev.ag_sb, 1, [[2 * NTT, 128], [2, NTT]])
        S.add("vector", lambda eng: eng.tensor_scalar(
            vv[:, 0:NTT], vview, EPS, None, op0=ALU.add), inc="i2a", sem=ev.sV)
        S.add("vector", lambda eng: eng.reciprocal(vv[:, NTT:2 * NTT], vv[:, 0:NTT]),
              waits=["i2a"], inc="i2b", sem=ev.sV)
        S.add("vector", lambda eng: eng.tensor_tensor(
            vv[:, 2 * NTT:3 * NTT], vview, vv[:, NTT:2 * NTT], op=ALU.mult),
            waits=["i2b"], inc="i2c", sem=ev.sV)
        S.add("scalar", lambda eng: eng.activation(
            vv[:, 3 * NTT:4 * NTT], vv[:, 2 * NTT:3 * NTT], AF.Ln, bias=ev.kc_sb[:, 0:1]),
            waits=["i2c"], inc="i2d", sem=ev.sA)
        S.add("scalar", lambda eng: eng.activation(
            ev.istd_sb[:, NTT:2 * NTT], vv[:, 3 * NTT:4 * NTT], AF.Exp, scale=-0.5),
            waits=["i2d"], inc="i2e", sem=ev.sA)

        for tt in range(NTT):
            w = []
            if tt >= 2:
                w = [f"ts{tt - 2}"]
            S.dma("sync", xin[tt % 2][:], src[tt * 128:(tt + 1) * 128, :],
                  waits=w, inc=f"xl{tt}", sem=ev.sL)
            w = [f"xl{tt}"]
            if tt >= 2:
                w += [f"p2{tt - 2}", f"sd{tt - 2}"]
            S.add("vector", lambda eng, tt=tt: eng.tensor_scalar(
                tn[tt % 2][:], xin[tt % 2][:], ev.ag_sb[:, 2 * tt:2 * tt + 1],
                ev.istd_sb[:, tt:tt + 1], op0=ALU.subtract, op1=ALU.mult),
                waits=w, inc=f"ts{tt}", sem=ev.sV)
            S.dma("scalar", tm_dst[tt * 128:(tt + 1) * 128, :], tn[tt % 2][:],
                  waits=[f"ts{tt}"], inc=f"sd{tt}", sem=ev.sS)
            w = [f"ts{tt}"]
            if tt == 0:
                w += ["gl", "bl", "i2e"]
            if tt >= 2:
                w += [f"hb{tt - 2}"]
            S.add("vector", lambda eng, tt=tt: eng.tensor_tensor(
                t2b[tt % 2][:], tn[tt % 2][:], ev.gb_sb[:, 0:D], op=ALU.mult),
                waits=w, inc=f"p2{tt}", sem=ev.sV)
            w = [f"p2{tt}"]
            if tt >= 2:
                w += [f"tr{tt - 2}"]
            S.add("vector", lambda eng, tt=tt: eng.scalar_tensor_tensor(
                h1[tt % 2][:], t2b[tt % 2][:], ev.istd_sb[:, NTT + tt:NTT + tt + 1],
                ev.gb_sb[:, D:2 * D], op0=ALU.mult, op1=ALU.add),
                waits=w, inc=f"hb{tt}", sem=ev.sV)
            w = [f"hb{tt}"]
            if tt >= 2:
                w += [f"cp{tt - 2}"]
            par = tt % 2
            for c in range(DC):
                S.add("tensor", lambda eng, tt=tt, c=c, par=par: eng.transpose(
                    ptr[2 * par + c // 8][:, 128 * (c % 8):128 * (c % 8 + 1)],
                    h1[par][:, 128 * c:128 * (c + 1)], ev.ident[:]),
                    waits=(w if c == 0 else ()),
                    inc=(f"tr{tt}" if c == DC - 1 else None), sem=ev.sP)
            w = [f"tr{tt}"]
            for half in range(2):
                S.add("vector", lambda eng, tt=tt, half=half, par=par: eng.tensor_copy(
                    bass.AP(fm_dst, (8 * half) * T + tt * 128,
                            [[DC * T, 128], [T, 8], [1, 128]]),
                    ptr[2 * par + half][:]),
                    waits=(w if half == 0 else ()),
                    inc=(f"cp{tt}" if half == 1 else None), sem=ev.sV)
        S.add("scalar", lambda eng: eng.nop(), waits=[f"sd{NTT - 1}", f"cp{NTT - 1}"], inc=None)
        S.run()


def phase_fmproj(ev, tag, W, bias_col0, dst, rope):
    nc = ev.nc
    with ExitStack() as ph:
        wf = [ph.enter_context(nc.sbuf_tensor(f"{tag}wf{i}", [128, 2048], F32)) for i in range(2)]
        wb = [ph.enter_context(nc.sbuf_tensor(f"{tag}wb{i}", [128, 2048], BF16)) for i in range(2)]
        ub = [ph.enter_context(nc.sbuf_tensor(f"{tag}ub{i}", [128, 512], BF16)) for i in range(4)]
        pp = [ph.enter_context(nc.psum_tensor(f"{tag}pp{i}", [128, 512], F32)) for i in range(8)]
        if rope:
            rotb = [ph.enter_context(nc.sbuf_tensor(f"{tag}ro{i}", [128, 512], BF16)) for i in range(2)]
            t1b = [ph.enter_context(nc.sbuf_tensor(f"{tag}t1{i}", [128, 512], BF16)) for i in range(2)]
            t2b = [ph.enter_context(nc.sbuf_tensor(f"{tag}t2{i}", [128, 512], BF16)) for i in range(2)]
            qb = [ph.enter_context(nc.sbuf_tensor(f"{tag}qb{i}", [128, 512], BF16)) for i in range(2)]
        S = Sched(ev, tag)
        for hc in range(H):
            w = []
            if hc >= 2:
                w = [f"wc{hc - 2}"]
            for dc in range(DC):
                S.dma("sync", wf[hc % 2][:, 128 * dc:128 * (dc + 1)],
                      W[128 * dc:128 * (dc + 1), 128 * hc:128 * (hc + 1)],
                      waits=(w if dc == 0 else ()),
                      inc=(f"wl{hc}" if dc == DC - 1 else None), sem=ev.sL)
            w = [f"wl{hc}"]
            if hc >= 2:
                w += [f"mm{hc - 2}_1"]
            S.add("scalar", lambda eng, hc=hc: eng.activation(
                wb[hc % 2][:], wf[hc % 2][:], AF.Identity),
                waits=w, inc=f"wc{hc}", sem=ev.sA)
            for half in range(2):
                for dc in range(DC):
                    w = []
                    if dc == 0:
                        w = [f"wc{hc}"]
                        prev = (hc * 2 + half) - 2
                        if prev >= 0:
                            w += [f"ev{prev // 2}_{prev % 2}_3"]
                    for q in range(4):
                        tc = half * 4 + q
                        S.add("tensor", lambda eng, hc=hc, dc=dc, q=q, tc=tc, half=half:
                              eng.matmul(pp[half * 4 + q][:],
                                         wb[hc % 2][:, 128 * dc:128 * (dc + 1)],
                                         ev.actT[:, dc * T + tc * 512:dc * T + (tc + 1) * 512],
                                         start=(dc == 0), stop=(dc == DC - 1)),
                              waits=(w if q == 0 else ()),
                              inc=(f"mm{hc}_{half}" if dc == DC - 1 and q == 3 else None),
                              sem=ev.sP)
                for q in range(4):
                    tc = half * 4 + q
                    sl = q % 2 if rope else q % 4
                    gq = (hc * 2 + half) * 4 + q
                    w = [f"mm{hc}_{half}"] if q == 0 else []
                    if rope:
                        if gq >= 2:
                            w += [f"rot{gq - 2}", f"t1_{gq - 2}"]
                    else:
                        prevs = gq - 4
                        if prevs >= 0:
                            w += [f"st{prevs}"]
                    S.add("scalar", lambda eng, half=half, q=q, hc=hc, sl=sl: eng.activation(
                        ub[sl if rope else q][:], pp[half * 4 + q][:], AF.Identity,
                        bias=ev.pcol_sb[:, bias_col0 + hc:bias_col0 + hc + 1]),
                        waits=w, inc=f"ev{hc}_{half}_{q}", sem=ev.sA)
                    if not rope:
                        S.dma("scalar", dst[hc, :, tc * 512:(tc + 1) * 512],
                              ub[q][:], waits=[f"ev{hc}_{half}_{q}"],
                              inc=f"st{gq}", sem=ev.sS)
                    else:
                        w2 = [f"ev{hc}_{half}_{q}"]
                        if gq >= 2:
                            w2 += [f"t2_{gq - 2}"]
                        S.add("gpsimd", lambda eng, sl=sl: eng.tensor_copy(
                            rotb[sl][0:64, :], ub[sl][64:128, :]), waits=w2)
                        S.add("gpsimd", lambda eng, sl=sl: eng.tensor_copy(
                            rotb[sl][64:128, :], ub[sl][0:64, :]),
                            inc=f"rot{gq}", sem=ev.sG)
                        w2 = [f"ev{hc}_{half}_{q}"]
                        if gq >= 2:
                            w2 += [f"sum{gq - 2}"]
                        S.add("vector", lambda eng, sl=sl, tc=tc: eng.tensor_tensor(
                            t1b[sl][:], ub[sl][:], ev.cos_sb[:, tc * 512:(tc + 1) * 512],
                            op=ALU.mult), waits=w2, inc=f"t1_{gq}", sem=ev.sV)
                        w2b = [f"rot{gq}"]
                        if gq >= 2:
                            w2b += [f"sum{gq - 2}"]
                        S.add("vector", lambda eng, sl=sl, tc=tc: eng.tensor_tensor(
                            t2b[sl][:], rotb[sl][:], ev.sinF_sb[:, tc * 512:(tc + 1) * 512],
                            op=ALU.mult), waits=w2b, inc=f"t2_{gq}", sem=ev.sV)
                        w2 = [f"t1_{gq}", f"t2_{gq}"]
                        if gq >= 2:
                            w2 += [f"st{gq - 2}"]
                        S.add("vector", lambda eng, sl=sl: eng.tensor_tensor(
                            qb[sl][:], t1b[sl][:], t2b[sl][:], op=ALU.add),
                            waits=w2, inc=f"sum{gq}", sem=ev.sV)
                        S.dma("scalar", dst[hc, :, tc * 512:(tc + 1) * 512],
                              qb[sl][:], waits=[f"sum{gq}"], inc=f"st{gq}", sem=ev.sS)
        S.add("scalar", lambda eng: eng.nop(), waits=[f"st{H * 8 - 1}"], inc=None)
        S.run()




def phase_attn(ev):
    nc = ev.nc
    with ExitStack() as ph:
        qT = ph.enter_context(nc.sbuf_tensor("at_q", [128, T], BF16))
        kT = ph.enter_context(nc.sbuf_tensor("at_k", [128, T], BF16))
        vT = ph.enter_context(nc.sbuf_tensor("at_v", [128, T], BF16))
        Vtm = ph.enter_context(nc.sbuf_tensor("at_V", [128, T], BF16))
        ones = ph.enter_context(nc.sbuf_tensor("at_1", [128, 128], BF16))
        msk = ph.enter_context(nc.sbuf_tensor("at_m", [128, 4 * 512], BF16))
        prb = [ph.enter_context(nc.sbuf_tensor(f"at_p{i}", [128, 512], BF16)) for i in range(4)]
        prm = [ph.enter_context(nc.sbuf_tensor(f"at_pm{i}", [128, 512], BF16)) for i in range(2)]
        tln = ph.enter_context(nc.sbuf_tensor("at_tl", [128, 512], F32))
        rec = ph.enter_context(nc.sbuf_tensor("at_rc", [128, 512], F32))
        ao = [ph.enter_context(nc.sbuf_tensor(f"at_o{i}", [128, 512], BF16)) for i in range(2)]
        ptb = [ph.enter_context(nc.psum_tensor(f"at_pt{i}", [128, 1024], BF16)) for i in range(2)]
        ps = [ph.enter_context(nc.psum_tensor(f"at_s{i}", [128, 512], F32)) for i in range(2)]
        pden = ph.enter_context(nc.psum_tensor("at_d", [128, 512], F32))
        ppv = [ph.enter_context(nc.psum_tensor(f"at_pv{i}", [128, 512], F32)) for i in range(2)]

        c2f = [ph.enter_context(nc.sbuf_tensor(f"at_cf{i}", [128, 2048], F32)) for i in range(2)]
        c2b = [ph.enter_context(nc.sbuf_tensor(f"at_cb{i}", [128, 2048], BF16)) for i in range(2)]

        S = Sched(ev, "attn")
        for fi in range(E * IC):
            e, ic = divmod(fi, IC)
            w = []
            if fi >= 2:
                w = [f"w2c{fi - 2}"]
            S.dma("sync", c2f[fi % 2][:], ev.we2[e, ic * 128:(ic + 1) * 128, :],
                  waits=w, inc=f"w2l{fi}", sem=ev.sL)
            w = [f"w2l{fi}"]
            if fi >= 2:
                w += [f"w2s{fi - 2}"]
            S.add("gpsimd", lambda eng, fi=fi: eng.tensor_copy(c2b[fi % 2][:], c2f[fi % 2][:]),
                  waits=w, inc=f"w2c{fi}", sem=ev.sG)
            S.dma("scalar", ev.we2bf_hbm[fi], c2b[fi % 2][:],
                  waits=[f"w2c{fi}"], inc=f"w2s{fi}", sem=ev.sS)
        for mi in range(4):
            S.dma("sync", msk[:, mi * 512:(mi + 1) * 512], ev.c_masks[mi],
                  inc=("mskl" if mi == 3 else None), sem=ev.sL)
        S.add("gpsimd", lambda eng: eng.memset(ones[:], 1.0), inc="ones", sem=ev.sG)

        for h in range(H):
            hp = f"h{h}"
            pv_ = f"h{h - 1}"
            wq = [] if h == 0 else [pv_ + "pq7_31"]
            S.dma("sync", qT[:], ev.q_hbm[h], waits=wq, inc=hp + "lq", sem=ev.sL)
            S.dma("sync", kT[:], ev.k_hbm[h], waits=(), inc=hp + "lk", sem=ev.sL)
            wv_ = [] if h == 0 else [pv_ + "vtr3"]
            S.dma("sync", vT[:], ev.v_hbm[h], waits=wv_, inc=hp + "lv", sem=ev.sL)
            for g in range(4):
                wg = [hp + "lv"]
                if h == 0 and g == 0:
                    wg += ["mskl"]
                if g >= 2:
                    wg += [hp + f"vcp{g - 2}"]
                elif h >= 1:
                    wg += [pv_ + f"vcp{g + 2}"]
                for j in range(8):
                    kt = g * 8 + j
                    S.add("tensor", lambda eng, g=g, j=j, kt=kt: eng.transpose(
                        ptb[g % 2][:, 128 * j:128 * (j + 1)],
                        vT[:, kt * 128:(kt + 1) * 128], ev.ident[:]),
                        waits=(wg if j == 0 else ()),
                        inc=(hp + f"vtr{g}" if j == 7 else None), sem=ev.sP)
                wc = [hp + f"vtr{g}"]
                if h >= 1 and g == 0:
                    wc += [pv_ + "pq7_31"]
                S.add("vector", lambda eng, g=g: eng.tensor_copy(
                    Vtm[:, g * 1024:(g + 1) * 1024], ptb[g % 2][:]),
                    waits=wc, inc=hp + f"vcp{g}", sem=ev.sV)
            for qc in range(NTC):
                nkt = 4 * (qc + 1)

                def emit_sc(kt, qc=qc, hp=hp):
                    ws = []
                    if kt == 0 and qc == 0:
                        ws = [hp + "lq", hp + "lk", hp + "vcp3"]
                    if kt >= 2:
                        ws += [hp + f"ex{qc}_{kt - 2}"]
                    elif qc >= 1:
                        ws += [hp + f"ex{qc - 1}_{4 * qc - 2 + kt}"]
                    S.add("tensor", lambda eng, kt=kt, qc=qc: eng.matmul(
                        ps[kt % 2][:], kT[:, kt * 128:(kt + 1) * 128],
                        qT[:, qc * 512:(qc + 1) * 512], start=True, stop=True),
                        waits=ws, inc=hp + f"sc{qc}_{kt}", sem=ev.sP)

                def emit_exp(kt, qc=qc, hp=hp, nkt=nkt):
                    diag = kt >= 4 * qc
                    we = [hp + f"sc{qc}_{kt}"]
                    if kt >= 4:
                        we += [hp + f"pq{qc}_{kt - 4}"]
                    elif qc >= 1:
                        we += [hp + f"pq{qc - 1}_{4 * qc - 4 + kt}"]
                    S.add("scalar", lambda eng, kt=kt, qc=qc: eng.activation(
                        prb[kt % 4][:], ps[kt % 2][:], AF.Exp, scale=RSQRT_HD),
                        waits=we, inc=hp + f"ex{qc}_{kt}", sem=ev.sA)
                    if diag:
                        wm = [hp + f"ex{qc}_{kt}"]
                        if kt >= 2:
                            qx = qc if kt - 2 >= 4 * qc else qc - 1
                            wm += [hp + f"pq{qx}_{kt - 2}"]
                        S.add("vector", lambda eng, kt=kt, qc=qc: eng.tensor_tensor(
                            prm[kt % 2][:], prb[kt % 4][:],
                            msk[:, (kt - 4 * qc) * 512:(kt - 4 * qc + 1) * 512],
                            op=ALU.mult),
                            waits=wm, inc=hp + f"mk{qc}_{kt}", sem=ev.sV)

                def emit_dv(kt, qc=qc, hp=hp, nkt=nkt):
                    diag = kt >= 4 * qc
                    use = prm[kt % 2] if diag else prb[kt % 4]
                    dep = hp + (f"mk{qc}_{kt}" if diag else f"ex{qc}_{kt}")
                    wd = [dep]
                    if kt == 0:
                        wd += [hp + f"dn{qc - 1}"] if qc >= 1 else ["ones"]
                        if qc >= 2:
                            wd += [hp + f"ao{qc - 2}"]
                    S.add("tensor", lambda eng, kt=kt, qc=qc, use=use: eng.matmul(
                        pden[:], ones[:], use[:], start=(kt == 0), stop=(kt == nkt - 1)),
                        waits=wd, inc=None, sem=ev.sP)
                    S.add("tensor", lambda eng, kt=kt, qc=qc, use=use: eng.matmul(
                        ppv[qc % 2][:], Vtm[:, kt * 128:(kt + 1) * 128], use[:],
                        start=(kt == 0), stop=(kt == nkt - 1)),
                        waits=(), inc=hp + f"pq{qc}_{kt}", sem=ev.sP)

                for kt in range(nkt):
                    emit_sc(kt)
                    emit_exp(kt)
                    if kt >= 1:
                        emit_dv(kt - 1)
                emit_dv(nkt - 1)
                S.add("scalar", lambda eng: eng.activation(tln[:], pden[:], AF.Ln),
                      waits=[hp + f"pq{qc}_{nkt - 1}"], inc=hp + f"dn{qc}", sem=ev.sA)
                S.add("scalar", lambda eng: eng.activation(rec[:], tln[:], AF.Exp, scale=-1.0),
                      waits=[hp + f"dn{qc}"], inc=hp + f"rc{qc}", sem=ev.sA)
                wo_ = [hp + f"rc{qc}"]
                if qc >= 2:
                    wo_ += [hp + f"st{qc - 2}"]
                S.add("vector", lambda eng, qc=qc: eng.scalar_tensor_tensor(
                    ao[qc % 2][:], ppv[qc % 2][:], 1.0, rec[:],
                    op0=ALU.mult, op1=ALU.mult),
                    waits=wo_, inc=hp + f"ao{qc}", sem=ev.sV)
                for j in range(4):
                    S.dma("scalar", ev.att_hbm[qc * 4 + j, :, h * 128:(h + 1) * 128],
                          ao[qc % 2][:, 128 * j:128 * (j + 1)],
                          waits=([hp + f"ao{qc}"] if j == 0 else ()),
                          inc=(hp + f"st{qc}" if j == 3 else None), sem=ev.sS)
        S.add("scalar", lambda eng: eng.nop(), waits=[f"h{H - 1}st7", f"w2s{E * IC - 1}"], inc=None)
        S.run()




def phase_gate(ev):
    nc = ev.nc
    with ExitStack() as ph:
        wgf = ph.enter_context(nc.sbuf_tensor("gt_wf", [128, 64], F32))
        wgb = ph.enter_context(nc.sbuf_tensor("gt_wb", [128, 64], BF16))
        sel = ph.enter_context(nc.sbuf_tensor("gt_sel", [4, 512], BF16))
        graw = ph.enter_context(nc.sbuf_tensor("gt_raw", [128, 128], F32))
        ee = ph.enter_context(nc.sbuf_tensor("gt_ee", [128, 128], F32))
        pe_ = ph.enter_context(nc.sbuf_tensor("gt_p", [128, 128], F32))
        te = ph.enter_context(nc.sbuf_tensor("gt_te", [128, 128], F32))
        comb = ph.enter_context(nc.sbuf_tensor("gt_cmb", [128, 128], F32))
        combb = ph.enter_context(nc.sbuf_tensor("gt_cbb", [128, 128], BF16))
        mkk = ph.enter_context(nc.sbuf_tensor("gt_mk", [128, 128], F32))
        tmp = ph.enter_context(nc.sbuf_tensor("gt_tmp", [128, 256], F32))
        combT = ev.combT_sb
        pgf = [ph.enter_context(nc.psum_tensor(f"gt_pg{i}", [128, 512], F32)) for i in range(4)]
        pg = [t[:, 0:4] for t in pgf]
        pctf = ph.enter_context(nc.psum_tensor("gt_pct", [128, 1024], BF16))
        pct = pctf[0:4, 0:128]
        pcb = [ph.enter_context(nc.psum_tensor(f"gt_pcb{i}", [128, 512], F32)) for i in range(2)]

        def ev4(t_, e):
            return bass.AP(t_, e, [[128, 128], [4, 32]])

        S = Sched(ev, "gate")
        for dc in range(DC):
            S.dma("sync", wgf[:, dc * 4:(dc + 1) * 4],
                  ev.w_gate[dc * 128:(dc + 1) * 128, :],
                  inc=(f"wl" if dc == DC - 1 else None), sem=ev.sL)
        S.dma("sync", sel[:], ev.c_sel, inc="sl", sem=ev.sL)
        S.add("gpsimd", lambda eng: eng.tensor_copy(wgb[:], wgf[:]),
              waits=["wl"], inc="wc", sem=ev.sG)
        for tt in range(NTT):
            for dc in range(DC):
                w = []
                if dc == 0:
                    w = ["wc"]
                    if tt >= 4:
                        w += [f"gev{tt - 4}"]
                S.add("tensor", lambda eng, tt=tt, dc=dc: eng.matmul(
                    pg[tt % 4], ev.actT[:, dc * T + tt * 128:dc * T + tt * 128 + 128],
                    wgb[:, dc * 4:(dc + 1) * 4],
                    start=(dc == 0), stop=(dc == DC - 1)),
                    waits=w, inc=(f"gmm{tt}" if dc == DC - 1 else None), sem=ev.sP)
            S.add("vector", lambda eng, tt=tt: eng.tensor_copy(
                graw[:, tt * 4:(tt + 1) * 4], pg[tt % 4]),
                waits=[f"gmm{tt}"], inc=f"gev{tt}", sem=ev.sV)
        S.add("scalar", lambda eng: eng.activation(ee[:], graw[:], AF.Exp),
              waits=[f"gev{NTT - 1}"], inc="eexp", sem=ev.sA)
        S.add("vector", lambda eng: eng.tensor_tensor(
            tmp[:, 0:32], ev4(ee, 0), ev4(ee, 1), op=ALU.add), waits=["eexp"],
            inc="s01", sem=ev.sV)
        S.add("vector", lambda eng: eng.tensor_tensor(
            tmp[:, 32:64], ev4(ee, 2), ev4(ee, 3), op=ALU.add), waits=[],
            inc="s23", sem=ev.sV)
        S.add("vector", lambda eng: eng.tensor_tensor(
            tmp[:, 64:96], tmp[:, 0:32], tmp[:, 32:64], op=ALU.add),
            waits=["s23"], inc="ssum", sem=ev.sV)
        S.add("vector", lambda eng: eng.reciprocal(tmp[:, 96:128], tmp[:, 64:96]),
              waits=["ssum"], inc="srec", sem=ev.sV)
        for e in range(E):
            S.add("vector", lambda eng, e=e: eng.tensor_tensor(
                ev4(pe_, e), ev4(ee, e), tmp[:, 96:128], op=ALU.mult),
                waits=(["srec"] if e == 0 else ()),
                inc=(f"pdone" if e == E - 1 else None), sem=ev.sV)
        S.add("vector", lambda eng: eng.nop(), waits=["pdone"], inc="pfence", sem=ev.sV)
        nf = 0
        for e in range(E):
            js = [j for j in range(E) if j != e]
            for idx, j in enumerate(js):
                op = ALU.is_ge if j < e else ALU.is_gt
                dstv = (lambda e=e: ev4(mkk, e))() if idx == 0 else tmp[:, 128 + 32 * (idx - 1):160 + 32 * (idx - 1)]
                S.add("vector", lambda eng, e=e, j=j, op=op, dstv=dstv: eng.tensor_tensor(
                    dstv, ev4(pe_, j), ev4(pe_, e), op=op),
                    waits=([f"cf{nf - 1}"] if nf > 0 else ["pfence"]),
                    inc=f"cf{nf}", sem=ev.sV)
                nf += 1
            for idx in range(2):
                S.add("vector", lambda eng, e=e, idx=idx: eng.tensor_tensor(
                    ev4(mkk, e), ev4(mkk, e), tmp[:, 128 + 32 * idx:160 + 32 * idx],
                    op=ALU.add),
                    waits=[f"cf{nf - 1}"], inc=f"cf{nf}", sem=ev.sV)
                nf += 1
        S.add("vector", lambda eng: eng.nop(), waits=[f"cf{nf - 1}"], inc="cnt", sem=ev.sV)
        S.add("vector", lambda eng: eng.tensor_scalar(
            mkk[:], mkk[:], 2.0, None, op0=ALU.is_lt), waits=["cnt"], inc="mlt", sem=ev.sV)
        S.add("scalar", lambda eng: eng.activation(te[:], pe_[:], AF.Exp),
              waits=["pdone"], inc="sexp", sem=ev.sA)
        S.add("vector", lambda eng: eng.tensor_tensor(te[:], te[:], mkk[:], op=ALU.mult),
              waits=["sexp", "mlt"], inc="temask", sem=ev.sV)
        S.add("vector", lambda eng: eng.tensor_tensor(
            tmp[:, 0:32], ev4(te, 0), ev4(te, 1), op=ALU.add), waits=["temask"],
            inc="t01", sem=ev.sV)
        S.add("vector", lambda eng: eng.tensor_tensor(
            tmp[:, 32:64], ev4(te, 2), ev4(te, 3), op=ALU.add), waits=[],
            inc="t23", sem=ev.sV)
        S.add("vector", lambda eng: eng.tensor_tensor(
            tmp[:, 64:96], tmp[:, 0:32], tmp[:, 32:64], op=ALU.add),
            waits=["t23"], inc="tsum", sem=ev.sV)
        S.add("vector", lambda eng: eng.reciprocal(tmp[:, 96:128], tmp[:, 64:96]),
              waits=["tsum"], inc="trec", sem=ev.sV)
        for e in range(E):
            S.add("vector", lambda eng, e=e: eng.tensor_tensor(
                ev4(comb, e), ev4(te, e), tmp[:, 96:128], op=ALU.mult),
                waits=(["trec"] if e == 0 else ()),
                inc=("cdone" if e == E - 1 else None), sem=ev.sV)
        S.add("gpsimd", lambda eng: eng.tensor_copy(combb[:], comb[:]),
              waits=["cdone"], inc="ccast", sem=ev.sG)
        for tt in range(NTT):
            S.add("tensor", lambda eng, tt=tt: eng.transpose(
                pct, combb[:, tt * 4:(tt + 1) * 4], ev.ident[:]),
                waits=(["ccast"] if tt == 0 else [f"ctc{tt - 1}"]),
                inc=f"ctr{tt}", sem=ev.sP)
            S.add("vector", lambda eng, tt=tt: eng.tensor_copy(
                combT[:, tt * 128:(tt + 1) * 128], pct),
                waits=[f"ctr{tt}"], inc=f"ctc{tt}", sem=ev.sV)
        for e in range(E):
            for tc in range(NTC):
                i = e * NTC + tc
                w = [f"ctc{NTT - 1}"] if i == 0 else []
                w += ["sl"] if i == 0 else []
                if i >= 2:
                    w += [f"cbc{i - 2}"]
                S.add("tensor", lambda eng, e=e, tc=tc, i=i: eng.matmul(
                    pcb[i % 2][:], sel[:, e * 128:(e + 1) * 128],
                    combT[:, tc * 512:(tc + 1) * 512], start=True, stop=True),
                    waits=w, inc=f"cbm{i}", sem=ev.sP)
                S.add("vector", lambda eng, e=e, tc=tc, i=i: eng.tensor_copy(
                    ev.cb_sb[e][:, tc * 512:(tc + 1) * 512], pcb[i % 2][:]),
                    waits=[f"cbm{i}"], inc=f"cbc{i}", sem=ev.sV)
        S.add("vector", lambda eng: eng.nop(), waits=[f"cbc{E * NTC - 1}"], inc=None)
        S.run()


def phase_moe_up(ev):
    nc = ev.nc
    with ExitStack() as ph:
        wf = [ph.enter_context(nc.sbuf_tensor(f"mew{i}", [128, 2048], F32)) for i in range(2)]
        wb = [ph.enter_context(nc.sbuf_tensor(f"meb{i}", [128, 2048], BF16)) for i in range(2)]
        tb = [ph.enter_context(nc.sbuf_tensor(f"met{i}", [128, 512], BF16)) for i in range(2)]
        gb = [ph.enter_context(nc.sbuf_tensor(f"meg{i}", [128, 512], BF16)) for i in range(2)]
        pp = [ph.enter_context(nc.psum_tensor(f"mep{i}", [128, 512], F32)) for i in range(8)]
        S = Sched(ev, "meup")
        for fi in range(E * IC):
            e, ic = divmod(fi, IC)
            w = []
            if fi >= 2:
                w = [f"ct{fi - 2}"]
            for dc in range(DC):
                S.dma("sync", wf[fi % 2][:, 128 * dc:128 * (dc + 1)],
                      ev.we1[e, 128 * dc:128 * (dc + 1), 128 * ic:128 * (ic + 1)],
                      waits=(w if dc == 0 else ()),
                      inc=(f"wl{fi}" if dc == DC - 1 else None), sem=ev.sL)
            w = [f"wl{fi}"]
            if fi >= 2:
                w += [f"mm{fi - 2}_1"]
            S.add("gpsimd", lambda eng, fi=fi: eng.tensor_copy(wb[fi % 2][:], wf[fi % 2][:]),
                  waits=w, inc=f"ct{fi}", sem=ev.sG)
            for half in range(2):
                for dc in range(DC):
                    w = []
                    if dc == 0:
                        w = [f"ct{fi}"]
                        if fi >= 1:
                            w += [f"ge{fi - 1}_{half}_3"]
                    for q in range(4):
                        tc = half * 4 + q
                        S.add("tensor", lambda eng, fi=fi, dc=dc, q=q, tc=tc, half=half:
                              eng.matmul(pp[half * 4 + q][:],
                                         wb[fi % 2][:, 128 * dc:128 * (dc + 1)],
                                         ev.actT[:, dc * T + tc * 512:dc * T + (tc + 1) * 512],
                                         start=(dc == 0), stop=(dc == DC - 1)),
                              waits=(w if q == 0 else ()),
                              inc=(f"mm{fi}_{half}" if dc == DC - 1 and q == 3 else None),
                              sem=ev.sP)
                for q in range(4):
                    tc = half * 4 + q
                    gq = (fi * 2 + half) * 4 + q
                    w = [f"mm{fi}_{half}"] if q == 0 else []
                    if gq >= 2:
                        w += [f"wg{gq - 2}"]
                    S.add("scalar", lambda eng, fi=fi, q=q, half=half, e=e, ic=ic:
                          eng.activation(tb[gq % 2 if False else (half * 4 + q) % 2][:],
                                         pp[half * 4 + q][:], AF.Gelu,
                                         bias=ev.pcol_sb[:, 128 + 16 * e + ic:129 + 16 * e + ic]),
                          waits=w, inc=f"ge{fi}_{half}_{q}", sem=ev.sA)
                    w = [f"ge{fi}_{half}_{q}"]
                    if gq >= 2:
                        w += [f"gs{gq - 2}"]
                    S.add("vector", lambda eng, e=e, tc=tc, half=half, q=q: eng.tensor_tensor(
                        gb[(half * 4 + q) % 2][:], tb[(half * 4 + q) % 2][:],
                        ev.cb_sb[e][:, tc * 512:(tc + 1) * 512], op=ALU.mult),
                        waits=w, inc=f"wg{gq}", sem=ev.sV)
                    S.dma("scalar", ev.ge_hbm[fi, :, tc * 512:(tc + 1) * 512],
                          gb[(half * 4 + q) % 2][:],
                          waits=[f"wg{gq}"], inc=f"gs{gq}", sem=ev.sS)
        S.add("scalar", lambda eng: eng.nop(), waits=[f"gs{E * IC * 8 - 1}"], inc=None)
        S.run()


def phase_moe_down(ev):
    nc = ev.nc
    NTC_ = 8
    with ExitStack() as ph:
        mc = ph.enter_context(nc.sbuf_tensor("mom", [128, 64 * 512], BF16))
        w2t = [ph.enter_context(nc.sbuf_tensor(f"mow{i}", [128, 512], BF16)) for i in range(8)]
        combT = ev.combT_sb
        be2b = ph.enter_context(nc.sbuf_tensor("mobe", [4, D], BF16))
        be2f = ph.enter_context(nc.sbuf_tensor("mobf", [4, D], F32))
        r1 = [ph.enter_context(nc.sbuf_tensor(f"mor1_{i}", [128, 512], F32)) for i in range(3)]
        r2 = [ph.enter_context(nc.sbuf_tensor(f"mor2_{i}", [128, 512], F32)) for i in range(3)]
        tband = [ph.enter_context(nc.sbuf_tensor(f"mot{i}", [128, 512], F32)) for i in range(3)]
        sband = [ph.enter_context(nc.sbuf_tensor(f"mos{i}", [128, 512], F32)) for i in range(3)]
        pp = [ph.enter_context(nc.psum_tensor(f"mop{i}", [128, 512], F32)) for i in range(8)]
        tmpv = ph.enter_context(nc.sbuf_tensor("motv", [128, NTT], F32))
        S = Sched(ev, "modn")
        S.dma("sync", be2f[:], ev.p_be2, inc="bel", sem=ev.sL)
        S.add("gpsimd", lambda eng: eng.tensor_copy(be2b[:], be2f[:]),
              waits=["bel"], inc="bec", sem=ev.sG)
        slot = 0
        for tc in range(NTC_):
            for fc in range(64):
                w = []
                if fc == 0 and tc >= 1:
                    gl = ((tc - 1) * 4 + 3) * 64 + 63
                    w = [f"mm_w{gl}"]
                S.dma("sync", mc[:, fc * 512:(fc + 1) * 512],
                      ev.ge_hbm[fc, :, tc * 512:(tc + 1) * 512],
                      waits=w, inc=(f"ml{tc}" if fc == 63 else None), sem=ev.sL)
            for og in range(4):
                gog = tc * 4 + og
                for fc in range(64):
                    gi = gog * 64 + fc
                    w = []
                    if gi >= 8:
                        w = [f"mm_w{gi - 7}"]
                    S.dma("scalar" if fc % 2 else "sync", w2t[gi % 8][:],
                          ev.we2bf_hbm[fc, :, og * 512:(og + 1) * 512],
                          waits=w, inc=f"wt{gi}", sem=(ev.sS if fc % 2 else ev.sL))
                    for tt in range(4):
                        w2 = []
                        if fc == 0 and tt == 0:
                            w2 = [f"ml{tc}", f"wt{gi}"]
                            if gog >= 2:
                                w2 += [f"pl_{(gog - 1) * 4 + 0}"]
                        elif tt == 0:
                            w2 = [f"wt{gi}"]
                        S.add("tensor", lambda eng, gog=gog, fc=fc, tt=tt, gi=gi:
                              eng.matmul(pp[(gog % 2) * 4 + tt][:],
                                         mc[:, fc * 512 + tt * 128:fc * 512 + tt * 128 + 128],
                                         w2t[gi % 8][:],
                                         start=(fc == 0), stop=False),
                              waits=w2,
                              inc=(f"mm_w{gi}" if tt == 3 else None),
                              sem=ev.sP)
                for tt in range(4):
                    gtt = tc * 4 + tt
                    w2 = [f"mm_w{gog * 64 + 63}"]
                    if tt == 0:
                        w2 += ["bec"]
                    S.add("tensor", lambda eng, gog=gog, og=og, tt=tt, gtt=gtt: eng.matmul(
                        pp[(gog % 2) * 4 + tt][:],
                        combT[0:4, gtt * 128:(gtt + 1) * 128],
                        be2b[0:4, og * 512:(og + 1) * 512],
                        start=False, stop=True),
                        waits=w2, inc=(f"mmb{gog}" if tt == 3 else None), sem=ev.sP)
                for tt in range(4):
                    gtt = tc * 4 + tt
                    sl = slot % 3
                    S.dma("sync", r1[sl][:],
                          ev.x3_hbm[gtt * 128:(gtt + 1) * 128, og * 512:(og + 1) * 512],
                          waits=([f"pl_{slot - 3}"] if slot >= 3 else ()),
                          inc=f"rl1_{slot}", sem=ev.sL)
                    S.dma("sync", r2[sl][:],
                          ev.xA_hbm[gtt * 128:(gtt + 1) * 128, og * 512:(og + 1) * 512],
                          waits=([f"f2_{slot - 3}"] if slot >= 3 else ()),
                          inc=f"rl2_{slot}", sem=ev.sL)
                    w = [f"mmb{gog}", f"rl1_{slot}"]
                    if slot >= 3:
                        w += [f"pl_{slot - 3}"]
                    S.add("vector", lambda eng, gog=gog, tt=tt, sl=sl: eng.scalar_tensor_tensor(
                        tband[sl][:], pp[(gog % 2) * 4 + tt][:], 1.0, r1[sl][:],
                        op0=ALU.mult, op1=ALU.add),
                        waits=w, inc=f"pl_{slot}", sem=ev.sV)
                    w = [f"pl_{slot}", f"rl2_{slot}"]
                    if slot >= 3:
                        w += [f"sd_{slot - 3}", f"bs_{slot - 3}"]
                    S.add("gpsimd", lambda eng, sl=sl: eng.tensor_tensor(
                        sband[sl][:], tband[sl][:], r2[sl][:], op=ALU.add),
                        waits=w, inc=f"f2_{slot}", sem=ev.sG)
                    S.add("vector", lambda eng, gtt=gtt, og=og, sl=sl: eng.bn_stats(
                        ev.stat_sb[:, gtt * 24 + 6 * og:gtt * 24 + 6 * og + 6], sband[sl][:]),
                        waits=[f"f2_{slot}"], inc=f"bs_{slot}", sem=ev.sV)
                    S.dma("scalar", ev.s2_hbm[gtt * 128:(gtt + 1) * 128,
                                              og * 512:(og + 1) * 512],
                          sband[sl][:], waits=[f"f2_{slot}"], inc=f"sd_{slot}", sem=ev.sS)
                    slot += 1
        S.add("vector", lambda eng: eng.nop(), waits=[f"bs_{slot - 1}"],
              inc="bsall", sem=ev.sV)
        for tt in range(NTT):
            S.add("vector", lambda eng, tt=tt: eng.bn_aggr(
                ev.ag_sb[:, 2 * tt:2 * tt + 2], ev.stat_sb[:, tt * 24:tt * 24 + 24]),
                waits=(["bsall"] if tt == 0 else ()),
                inc=("agg" if tt == NTT - 1 else None), sem=ev.sV)
        S.add("scalar", lambda eng: eng.activation(
            tmpv[:], bass.AP(ev.ag_sb, 1, [[2 * NTT, 128], [2, NTT]]), AF.Ln,
            bias=ev.kc_sb[:, 0:1]), waits=["agg"], inc="lnt", sem=ev.sA)
        S.add("scalar", lambda eng: eng.activation(
            ev.istd_sb[:, 0:NTT], tmpv[:], AF.Exp, scale=-0.5),
            waits=["lnt"], inc="istd", sem=ev.sA)
        S.add("scalar", lambda eng: eng.nop(), waits=[f"sd_{slot - 1}"], inc=None)
        S.run()


def phase_final(ev):
    nc = ev.nc
    with ExitStack() as ph:
        xin = [ph.enter_context(nc.sbuf_tensor(f"fnx{i}", [128, D], F32)) for i in range(3)]
        tn = [ph.enter_context(nc.sbuf_tensor(f"fnt{i}", [128, D], F32)) for i in range(3)]
        S = Sched(ev, "fin")
        for tt in range(NTT):
            w = []
            if tt >= 3:
                w = [f"ts{tt - 3}"]
            S.dma("sync", xin[tt % 3][:], ev.s2_hbm[tt * 128:(tt + 1) * 128, :],
                  waits=w, inc=f"xl{tt}", sem=ev.sL)
            w = [f"xl{tt}"]
            if tt >= 3:
                w += [f"sd{tt - 3}"]
            S.add("vector", lambda eng, tt=tt: eng.tensor_scalar(
                tn[tt % 3][:], xin[tt % 3][:], ev.ag_sb[:, 2 * tt:2 * tt + 1],
                ev.istd_sb[:, tt:tt + 1], op0=ALU.subtract, op1=ALU.mult),
                waits=w, inc=f"ts{tt}", sem=ev.sV)
            S.dma("scalar", ev.out[tt * 128:(tt + 1) * 128, :], tn[tt % 3][:],
                  waits=[f"ts{tt}"], inc=f"sd{tt}", sem=ev.sS)
        S.add("scalar", lambda eng: eng.nop(), waits=[f"sd{NTT - 1}"], inc=None)
        S.run()


def build_nc(upto=99, dbg=()):
    nc = bass.Bass("TRN2", target_bir_lowering=False)
    ev = Env()
    ev.nc = nc
    ev.dbg = set(dbg)

    def xin(name, shape, dt=F32):
        return nc.dram_tensor(name, shape, dt, kind="ExternalInput").ap()

    ev.x = xin("x", [T, D])
    ev.in_w = xin("in_w", [D, D])
    ev.A = xin("Am", [H, 128, 128])
    ev.Bm = xin("Bmm", [H, 128, 128])
    ev.Cm = xin("Cmm", [H, 128, 128])
    ev.Dm = xin("Dmm", [H, 128, 128])
    ev.out_w = xin("out_w", [D, D])
    ev.w1 = xin("w1", [D, FF])
    ev.w2 = xin("w2", [FF, D])
    ev.wq = xin("wq", [D, D])
    ev.wk = xin("wk", [D, D])
    ev.wv = xin("wv", [D, D])
    ev.wo = xin("wo", [D, D])
    ev.w_gate = xin("w_gate", [D, E])
    ev.we1 = xin("we1", [E, D, EI_ := 2048])
    ev.we2 = xin("we2", [E, 2048, D])
    for name, shape, dt in [
        ("c_ident", [128, 128], BF16), ("c_cos", [128, T], BF16), ("c_sel", [4, 512], BF16),
        ("c_sinF", [128, T], BF16), ("c_masks", [4, 128, 512], BF16),
        ("p_ln1_g", [128, D], F32), ("p_ln1_b", [128, D], F32),
        ("p_ln2_g", [128, D], F32), ("p_ln2_b", [128, D], F32),
        ("p_fln1_g", [128, D], F32), ("p_fln1_b", [128, D], F32),
        ("p_fln2_g", [128, D], F32), ("p_fln2_b", [128, D], F32),
        ("p_in_b", [128, DC], F32), ("p_out_b", [128, D], F32),
        ("p_b1", [128, FC], F32), ("p_b2", [128, D], F32),
        ("p_bq", [128, DC], F32), ("p_bk", [128, DC], F32),
        ("p_bv", [128, DC], F32), ("p_bo", [128, D], F32),
        ("p_be1", [E, 128, 16], F32), ("p_be2", [E, D], F32),
    ]:
        setattr(ev, name, xin(name, shape, dt))

    ev.out = nc.dram_tensor("out", [T, D], F32, kind="ExternalOutput").ap()

    def scratch(name, shape, dt):
        return nc.dram_tensor(name, shape, dt, kind="Internal").ap()

    ev.u_hbm = scratch("u_hbm", [H, 128, T], BF16)
    ev.y_hbm = scratch("y_hbm", [NTT, 128, H * 128], BF16)
    ev.x1_hbm = scratch("x1_hbm", [T, D], F32)
    ev.s1_hbm = scratch("s1_hbm", [T, D], F32)
    ev.xA_hbm = scratch("xA_hbm", [T, D], F32)
    ev.mid_hbm = scratch("mid_hbm", [FC, 128, T], BF16)
    ev.q_hbm = scratch("q_hbm", [H, 128, T], BF16)
    ev.k_hbm = scratch("k_hbm", [H, 128, T], BF16)
    ev.v_hbm = scratch("v_hbm", [H, 128, T], BF16)
    ev.att_hbm = scratch("att_hbm", [NTT, 128, H * 128], BF16)
    ev.x3_hbm = scratch("x3_hbm", [T, D], F32)
    ev.ge_hbm = scratch("ge_hbm", [E * IC, 128, T], BF16)
    ev.w2bf_hbm = scratch("w2bf_hbm", [FC, 128, D], BF16)
    ev.we2bf_hbm = scratch("we2bf_hbm", [E * IC, 128, D], BF16)
    ev.s2_hbm = scratch("s2_hbm", [T, D], F32)

    ev.dbg_outs = {}
    for name in ev.dbg:
        src = getattr(ev, name)
        t = src.tensor
        ev.dbg_outs[name] = nc.dram_tensor("dbg_" + name, list(t.shape), t.dtype,
                                           kind="ExternalOutput").ap()

    with ExitStack() as st:
        ev.st = st
        ev.sL = Sem(st, nc, "sL")
        ev.sS = Sem(st, nc, "sS")
        ev.sP = Sem(st, nc, "sP")
        ev.sV = Sem(st, nc, "sV")
        ev.sA = Sem(st, nc, "sA")
        ev.sG = Sem(st, nc, "sG")
        ev.sX = Sem(st, nc, "sX")

        ev.ident = st.enter_context(nc.sbuf_tensor("ident", [128, 128], BF16))
        ev.stat_sb = st.enter_context(nc.sbuf_tensor("stat_sb", [128, 24 * NTT], F32))
        ev.ag_sb = st.enter_context(nc.sbuf_tensor("ag_sb", [128, 2 * NTT], F32))
        ev.istd_sb = st.enter_context(nc.sbuf_tensor("istd_sb", [128, 2 * NTT], F32))
        ev.pcol_sb = st.enter_context(nc.sbuf_tensor("pcol_sb", [128, 256], F32))
        ev.kc_sb = st.enter_context(nc.sbuf_tensor("kc_sb", [128, 4], F32))

        import os
        if os.environ.get("MK_DBG_PSUM"):
            ev.dbg_psum = nc.dram_tensor("dbg_psum", [128, 1024], F32,
                                         kind="ExternalOutput").ap()
            ev.dbg_psum_sb = st.enter_context(nc.sbuf_tensor("dbgps", [128, 1024], F32))
        else:
            ev.dbg_psum = None
        phase_prep(ev)

        if upto >= 1:
            with ExitStack() as grp:
                ev.actT = grp.enter_context(nc.sbuf_tensor("actT", [128, DC * T], BF16))
                phase_ln_stats(ev, "l1s", ev.x)
                phase_ln_apply(ev, "l1a", ev.x, ("p_ln1_g", "p_ln1_b"), ev.actT)
                if "actT" in ev.dbg:
                    dbg_actT = nc.dram_tensor("dbg_actT", [128, DC * T], BF16,
                                              kind="ExternalOutput").ap()
                    with nc.Block() as blk:
                        @blk.scalar
                        def _(scalar):
                            ev.sS.inc(scalar.dma_start(dbg_actT, ev.actT[:]), 16)
                            scalar.wait_ge(ev.sS.h, ev.sS.n)
                if upto >= 2:
                    phase_inproj(ev)
            if upto >= 3:
                phase_ssm(ev)
            if upto >= 4:
                phase_proj_tm(ev, "opj", ("hbm_tiles", ev.y_hbm), ev.out_w, D,
                              [ev.x], ev.x1_hbm, ev.p_out_b)
            if upto >= 5:
                with ExitStack() as grp:
                    ev.actT = grp.enter_context(nc.sbuf_tensor("actT2", [128, DC * T], BF16))
                    phase_ln_apply(ev, "l2a", ev.x1_hbm, ("p_ln2_g", "p_ln2_b"), ev.actT)
                    if upto >= 6:
                        phase_mlp_up(ev)
                if upto >= 7:
                    phase_mlp_down(ev)
            if upto >= 8:
                with ExitStack() as grp:
                    ev.actT = grp.enter_context(nc.sbuf_tensor("actT3", [128, DC * T], BF16))
                    phase_ln_merged(ev, "ol1", ev.s1_hbm, ("p_fln1_g", "p_fln1_b"),
                                    ev.actT, ev.xA_hbm)
                    if upto >= 9:
                        ev.cos_sb = grp.enter_context(nc.sbuf_tensor("cos_sb", [128, T], BF16))
                        ev.sinF_sb = grp.enter_context(nc.sbuf_tensor("sinF_sb", [128, T], BF16))
                        with nc.Block() as blk:
                            @blk.sync
                            def _(sync):
                                ev.sL.inc(sync.dma_start(ev.cos_sb[:], ev.c_cos), 16)
                                ev.sL.inc(sync.dma_start(ev.sinF_sb[:], ev.c_sinF), 16)
                                sync.wait_ge(ev.sL.h, ev.sL.n)
                        phase_fmproj(ev, "pq", ev.wq, 16, ev.q_hbm, rope=True)
                        phase_fmproj(ev, "pk", ev.wk, 32, ev.k_hbm, rope=True)
                        phase_fmproj(ev, "pv", ev.wv, 48, ev.v_hbm, rope=False)
            if upto >= 10:
                phase_attn(ev)
            if upto >= 11:
                phase_proj_tm(ev, "wop", ("hbm_tiles", ev.att_hbm), ev.wo, D,
                              [ev.xA_hbm], ev.x3_hbm, ev.p_bo)
            if upto >= 12:
                with ExitStack() as grp:
                    ev.combT_sb = grp.enter_context(nc.sbuf_tensor("combT", [4, T], BF16))
                    with ExitStack() as g2:
                        ev.actT = g2.enter_context(nc.sbuf_tensor("actT4", [128, DC * T], BF16))
                        phase_ln_apply(ev, "f2a", ev.x3_hbm, ("p_fln2_g", "p_fln2_b"), ev.actT)
                        ev.cb_sb = [g2.enter_context(nc.sbuf_tensor(f"cb{e}", [128, T], BF16))
                                    for e in range(E)]
                        if upto >= 13:
                            phase_gate(ev)
                        if upto >= 14:
                            phase_moe_up(ev)
                    if upto >= 15:
                        phase_moe_down(ev)
            if upto >= 16:
                phase_final(ev)

        _finish(ev)
    return nc, ev


def _finish(ev):
    nc = ev.nc
    with nc.Block() as block:
        @block.scalar
        def _(scalar):
            for name, dst in ev.dbg_outs.items():
                src = getattr(ev, name)
                ev.sS.inc(scalar.dma_start(dst, src), 16)
            scalar.wait_ge(ev.sS.h, ev.sS.n)


def _in_maps(inputs):
    hc = host_constants()
    pc = param_constants(inputs)
    base = dict(
        in_w=inputs["L0_attn_in_w"], Am=inputs["L0_attn_A"], Bmm=inputs["L0_attn_Bm"],
        Cmm=inputs["L0_attn_Cm"], Dmm=inputs["L0_attn_Dm"], out_w=inputs["L0_attn_out_w"],
        w1=inputs["L0_attn_w1"], w2=inputs["L0_attn_w2"],
        wq=inputs["L0_ffn_wq"], wk=inputs["L0_ffn_wk"], wv=inputs["L0_ffn_wv"],
        wo=inputs["L0_ffn_wo"], w_gate=inputs["L0_ffn_w_gate"],
        we1=inputs["L0_ffn_we1"], we2=inputs["L0_ffn_we2"],
    )
    base.update(hc)
    base.update(pc)
    maps = []
    xfull = np.ascontiguousarray(np.asarray(inputs["L0_attn_x"], np.float32))
    for b in range(8):
        m = dict(base)
        m["x"] = xfull[b]
        maps.append(m)
    return maps


def kernel(**inputs):
    nc, ev = build_nc()
    maps = _in_maps(inputs)
    res = run_bass_kernel_spmd(nc, maps, core_ids=list(range(8)))
    return np.stack([r["out"] for r in res.results], 0)
